# version 01 (final)
# baseline (speedup 1.0000x reference)
import numpy as np
import ml_dtypes
import concourse.bass as bass
import concourse.mybir as mybir
from concourse.bass_utils import run_bass_kernel_spmd

F32 = mybir.dt.float32
BF16 = mybir.dt.bfloat16
I32 = mybir.dt.int32
AF = mybir.ActivationFunctionType
ALU = mybir.AluOpType

B, L, D = 4, 4096, 2048
HFF = 8192
N_CORES = 8
TOK = B * L
TPC = TOK // N_CORES
EPS = 1e-5
ENGINES = ("sync", "tensor", "vector", "scalar", "gpsimd")


def pack_fm(xT):
    D_, T_ = xT.shape
    return np.ascontiguousarray(xT.reshape(D_ // 128, 128, T_).transpose(1, 0, 2))


def unpack_fm(a):
    p, nd, T_ = a.shape
    return np.ascontiguousarray(a.transpose(1, 0, 2).reshape(nd * p, T_))


def pack_lhsT(WT, slab):
    Di, Do = WT.shape
    return np.ascontiguousarray(
        WT.reshape(Di // 128, 128, Do // slab, slab).transpose(2, 1, 0, 3))


def hadamard(n):
    H = np.array([[1.0]], np.float32)
    while H.shape[0] < n:
        H = np.block([[H, H], [H, -H]])
    return H


class Prog:

    def __init__(self, nc, stack):
        self.nc = nc
        self.stack = stack
        self.q = {e: [] for e in ENGINES}
        self._sems = {}
        self.cnt = {}
        self._nbar = 0
        self._ord = {}
        self._last = {}
        for e in ("vector", "scalar"):
            self._ord[e] = self.sem(f"ord_{e}")

    def sem(self, name):
        h = self.stack.enter_context(self.nc.semaphore(name))
        self._sems[name] = h
        self.cnt[name] = 0
        return name

    def emit(self, eng, fn, inc=None, n=1):
        prev = None
        if eng in self._ord:
            prev = self._last.get(eng)
            if inc is None:
                inc = self._ord[eng]
                n = 1
        h = self._sems[inc] if inc is not None else None
        def wrapped(e, fn=fn, h=h, n=n, prev=prev):
            if prev is not None:
                e.wait_ge(prev[0], prev[1])
            r = fn(e)
            if h is not None:
                r.then_inc(h, n)
        self.q[eng].append(wrapped)
        if inc is not None:
            self.cnt[inc] += n
            if eng in self._ord:
                self._last[eng] = (self._sems[inc], self.cnt[inc])
            return self.cnt[inc]

    def dma(self, eng, out, in_, inc):
        return self.emit(eng, lambda e, o=out, i=in_: e.dma_start(out=o, in_=i),
                         inc=inc, n=16)

    def wait(self, eng, sem, val):
        h = self._sems[sem]
        self.q[eng].append(lambda e, h=h, v=val: e.wait_ge(h, v))

    def wait_now(self, eng, sem):
        self.wait(eng, sem, self.cnt[sem])

    def barrier(self):
        self._nbar += 1
        s = self.sem(f"bar{self._nbar}")
        h = self._sems[s]
        for e in ENGINES:
            if e == "vector":
                self.q[e].append(lambda x, h=h: x.engine_nop().then_inc(h, 1))
            else:
                self.q[e].append(lambda x, h=h: x.nop().then_inc(h, 1))
            self.cnt[s] += 1
        for e in ENGINES:
            self.wait(e, s, len(ENGINES))
        return s

    def build(self, block):
        for name in ENGINES:
            fns = self.q[name]
            def make(fns):
                def body(engine):
                    for f in fns:
                        f(engine)
                return body
            getattr(block, name)(make(fns))


def fm_ln(p, t_view, n_dic, width, ps_sum, ps_sq, ps_mu, ps_rstd, ones_sb,
          row_sb, sq_view, apply_out, d_total, s_t, s_sq, s_st, s_row, s_rep,
          cons=None):
    p.wait_now("scalar", s_t)
    for d in range(n_dic):
        p.emit("scalar", lambda e, d=d: e.activation(sq_view(d), t_view(d), AF.Square),
               inc=s_sq)
    p.wait_now("tensor", s_t)
    p.wait_now("tensor", s_sq)
    for d in range(n_dic):
        p.emit("tensor", lambda e, d=d: e.matmul(
            ps_sum, ones_sb[:, 0:1], t_view(d), start=(d == 0), stop=(d == n_dic - 1)))
        p.emit("tensor", lambda e, d=d: e.matmul(
            ps_sq, ones_sb[:, 0:1], sq_view(d), start=(d == 0), stop=(d == n_dic - 1)),
            inc=s_st)
    p.wait_now("vector", s_st)
    inv_d = 1.0 / d_total
    r = lambda j: row_sb[0:1, j, 0:width]
    p.emit("vector", lambda e: e.tensor_scalar_mul(r(0), ps_sum, inv_d))
    p.emit("vector", lambda e: e.tensor_scalar_mul(r(1), ps_sq, inv_d), inc=s_row)
    p.wait_now("scalar", s_row)
    p.emit("scalar", lambda e: e.activation(r(2), r(0), AF.Square), inc=s_row)
    p.wait_now("vector", s_row)
    p.emit("vector", lambda e: e.tensor_sub(r(3), r(1), r(2)), inc=s_row)
    p.wait_now("scalar", s_row)
    p.emit("scalar", lambda e: e.activation(r(4), r(3), AF.Ln, bias=cons["eps"]))
    p.emit("scalar", lambda e: e.activation(r(5), r(4), AF.Exp, scale=cons["neghalf"]), inc=s_row)
    p.wait_now("tensor", s_row)
    p.emit("tensor", lambda e: e.matmul(ps_mu, ones_sb[0:1, :], r(0), start=True, stop=True))
    p.emit("tensor", lambda e: e.matmul(ps_rstd, ones_sb[0:1, :], r(5), start=True, stop=True),
           inc=s_rep)
    p.wait_now("vector", s_rep)
    for d in range(n_dic):
        p.emit("vector", lambda e, d=d: e.tensor_sub(sq_view(d), t_view(d), ps_mu))
        p.emit("vector", lambda e, d=d: e.tensor_mul(sq_view(d), sq_view(d), ps_rstd))
        apply_out(d)



def build_l0ffn():
    from contextlib import ExitStack
    nc = bass.Bass("TRN2", target_bir_lowering=False, debug=False)
    xs = nc.dram_tensor("xs", [128, 16, TPC], F32, kind="ExternalInput").ap()
    ys = nc.dram_tensor("ys", [128, 16, TPC], F32, kind="ExternalInput").ap()
    w1p = nc.dram_tensor("w1p", [16, 128, 16, 512], F32, kind="ExternalInput").ap()
    w3p = nc.dram_tensor("w3p", [16, 128, 16, 512], F32, kind="ExternalInput").ap()
    w2p = nc.dram_tensor("w2p", [8, 128, 64, 256], F32, kind="ExternalInput").ap()
    hcon = nc.dram_tensor("hcon", [128, 128], F32, kind="ExternalInput").ap()
    s2o = nc.dram_tensor("s2o", [128, 16, TPC], F32, kind="ExternalOutput").ap()
    h1s = nc.dram_tensor("h1s", [64, 128, TPC], BF16, kind="Internal").ap()
    s1s = nc.dram_tensor("s1s", [128, 16, TPC], BF16, kind="Internal").ap()

    LNW = 256
    NLN = TPC // LNW
    SCALE = float(1.0 / np.sqrt(HFF))
    from contextlib import ExitStack
    with ExitStack() as st:
        p = Prog(nc, st)
        sb = lambda name, shape, dt: st.enter_context(nc.sbuf_tensor(name, shape, dt))
        ones_sb = sb("ones", [128, 128], F32)
        cons_sb = sb("cons", [1, 8], F32)
        cons = {"eps": cons_sb[0:1, 0:1], "neghalf": cons_sb[0:1, 1:2]}
        hc_sb = sb("hc", [128, 128], BF16)
        row_sb = sb("row", [1, 6, 256], F32)
        t_sb = sb("t", [128, 16, LNW], F32)
        sq_sb = sb("sq", [128, 16, LNW], F32)
        xbuf = sb("xbuf", [128, 2, LNW], F32)
        ybuf = sb("ybuf", [128, 2, LNW], F32)
        silu_sb = sb("silu", [128, 2, 512], BF16)
        gate_sb = sb("gate", [128, 2, 512], BF16)
        pool = sb("pool", [128, 65536], BF16)
        s1_sb = pool[:, 0:32768].rearrange("p (a b) -> p a b", a=16)
        w13_sb = pool[:, 32768:49152].rearrange("p (a b c) -> p a b c", a=2, b=16)
        h1_sb = pool[:, 49152:65536].rearrange("p (a b c) -> p a b c", a=2, b=4)
        hq_sb = pool[:, 0:32768].rearrange("p (a b) -> p a b", a=64)
        w2_sb = pool[:, 32768:49152].rearrange("p (a b) -> p a b", a=64)
        yq_sb = pool[:, 49152:57344].rearrange("p (a b) -> p a b", a=16)
        s1q_sb = pool[:, 57344:65536].rearrange("p (a b) -> p a b", a=16)
        ps = st.enter_context(nc.psum_tensor("ps", [128, 8, 512], F32))
        ps_sum = ps[0:1, 6, 0:LNW]
        ps_sq = ps[0:1, 5, 0:LNW]
        ps_mu = ps[:, 7, 0:LNW]
        ps_rstd = ps[:, 4, 0:LNW]

        s_ld = p.sem("s_ld"); s_t = p.sem("s_t"); s_sq = p.sem("s_sq")
        s_st = p.sem("s_st"); s_row = p.sem("s_row"); s_rep = p.sem("s_rep")
        s_s1 = p.sem("s_s1"); s_s1st = p.sem("s_s1st")
        s_w = p.sem("s_w"); s_wfree = p.sem("s_wfree")
        s_mm = p.sem("s_mm"); s_sil = p.sem("s_sil"); s_gate = p.sem("s_gate")
        s_h1mm = p.sem("s_h1mm"); s_h1 = p.sem("s_h1"); s_h1st = p.sem("s_h1st")
        s_hq = p.sem("s_hq"); s_bf = p.sem("s_bf")
        s_w2 = p.sem("s_w2"); s_cmm = p.sem("s_cmm"); s_cev = p.sem("s_cev")
        s_out = p.sem("s_out"); s_hc = p.sem("s_hc")

        p.emit("vector", lambda e: e.memset(ones_sb[:], 1.0))
        p.emit("vector", lambda e: e.memset(cons_sb[0:1, 0:1], float(EPS)))
        p.emit("vector", lambda e: e.memset(cons_sb[0:1, 1:2], -0.5), inc=s_t)
        p.wait("scalar", s_t, 1)
        p.dma("gpsimd", hc_sb[:], hcon[:], s_hc)
        p.wait("tensor", s_t, 1)
        p.wait("tensor", s_hc, 16)

        for lc in range(NLN):
            for dic in range(16):
                p.wait_now("gpsimd", s_t)
                p.dma("gpsimd", xbuf[:, dic % 2], xs[:, dic, lc * LNW:(lc + 1) * LNW], s_ld)
                p.dma("gpsimd", ybuf[:, dic % 2], ys[:, dic, lc * LNW:(lc + 1) * LNW], s_ld)
                p.wait_now("vector", s_ld)
                p.emit("vector", lambda e, d=dic: e.tensor_add(t_sb[:, d, 0:LNW], xbuf[:, d % 2], ybuf[:, d % 2]),
                       inc=s_t)

            def apply0(d, lc=lc):
                p.emit("vector", lambda e, d=d, lc=lc: e.tensor_copy(
                    s1_sb[:, d, lc * LNW:(lc + 1) * LNW], sq_sb[:, d, 0:LNW]), inc=s_s1)

            fm_ln(p, lambda d: t_sb[:, d, 0:LNW], 16, LNW, ps_sum, ps_sq, ps_mu,
                  ps_rstd, ones_sb, row_sb, lambda d: sq_sb[:, d, 0:LNW],
                  apply0, D, s_t, s_sq, s_st, s_row, s_rep, cons)
            p.wait_now("sync", s_s1)
            p.dma("sync", s1s[:, :, lc * LNW:(lc + 1) * LNW],
                  s1_sb[:, :, lc * LNW:(lc + 1) * LNW], s_s1st)
        p.wait_now("vector", s_s1st)
        p.barrier()

        for s in range(16):
            if s >= 1:
                p.wait("gpsimd", s_gate, 16 * s)
            p.dma("gpsimd", w13_sb[:, 0].rearrange("p a b -> p (a b)"),
                  w1p[s].rearrange("p a b -> p (a b)"), s_w)
            p.dma("gpsimd", w13_sb[:, 1].rearrange("p a b -> p (a b)"),
                  w3p[s].rearrange("p a b -> p (a b)"), s_w)

        kk = 0
        for s in range(16):
            p.wait("tensor", s_w, 32 * (s + 1))
            for ht in range(4):
                for lc in range(4):
                    bG, bL = (2 * kk) % 8, (2 * kk + 1) % 8
                    if kk >= 4:
                        p.wait("tensor", s_h1, kk - 3)
                        p.wait("tensor", s_gate, kk - 3)
                    for d in range(16):
                        p.emit("tensor", lambda e, ht=ht, lc=lc, d=d, bG=bG: e.matmul(
                            ps[:, bG], w13_sb[:, 0, d, ht * 128:(ht + 1) * 128],
                            s1_sb[:, d, lc * 512:(lc + 1) * 512],
                            start=(d == 0), stop=(d == 15)),
                            inc=(s_mm if d == 15 else None))
                    for d in range(16):
                        p.emit("tensor", lambda e, ht=ht, lc=lc, d=d, bL=bL: e.matmul(
                            ps[:, bL], w13_sb[:, 1, d, ht * 128:(ht + 1) * 128],
                            s1_sb[:, d, lc * 512:(lc + 1) * 512],
                            start=(d == 0), stop=(d == 15)),
                            inc=(s_mm if d == 15 else None))
                    p.wait("scalar", s_mm, 2 * kk + 1)
                    if kk >= 2:
                        p.wait("scalar", s_gate, kk - 1)
                    p.emit("scalar", lambda e, kk=kk, bG=bG: e.activation(
                        silu_sb[:, kk % 2], ps[:, bG], AF.Silu), inc=s_sil)
                    p.wait("vector", s_mm, 2 * kk + 2)
                    p.wait("vector", s_sil, kk + 1)
                    if kk >= 2:
                        p.wait("vector", s_h1mm, kk - 1)
                    p.emit("vector", lambda e, kk=kk, bL=bL: e.scalar_tensor_tensor(
                        gate_sb[:, kk % 2], ps[:, bL], SCALE, silu_sb[:, kk % 2],
                        ALU.mult, ALU.mult), inc=s_gate)
                    p.wait("tensor", s_gate, kk + 1)
                    p.wait("tensor", s_sil, kk + 1)
                    p.emit("tensor", lambda e, kk=kk, bG=bG: e.matmul(
                        ps[:, bG], hc_sb[:], gate_sb[:, kk % 2], start=True, stop=True),
                        inc=s_h1mm)
                    p.wait("vector", s_h1mm, kk + 1)
                    if s >= 2 and ht == 0 and lc == 0:
                        p.wait("vector", s_h1st, 64 * (s - 1))
                    p.emit("vector", lambda e, s=s, ht=ht, lc=lc, bG=bG: e.tensor_copy(
                        h1_sb[:, s % 2, ht, lc * 512:(lc + 1) * 512], ps[:, bG]), inc=s_h1)
                    kk += 1
            p.wait("sync", s_h1, 16 * (s + 1))
            p.wait("sync", s_h1st, 64 * s)
            for ht in range(4):
                p.dma("sync", h1s[4 * s + ht], h1_sb[:, s % 2, ht], s_h1st)


        p.barrier()

        NLQ = TPC // 512
        p.wait_now("gpsimd", s_h1st)
        p.wait_now("gpsimd", s_s1st)
        for lq in range(NLQ):
            if lq >= 1:
                p.wait("gpsimd", s_out, 32 * lq)
            base_hq = p.cnt[s_hq]
            for grp in range(8):
                p.dma("gpsimd", hq_sb[:, grp * 8:(grp + 1) * 8, :],
                      h1s[grp * 8:(grp + 1) * 8, :, lq * 512:(lq + 1) * 512]
                      .rearrange("a p b -> p a b"), s_hq)
            p.dma("gpsimd", s1q_sb[:], s1s[:, :, lq * 512:(lq + 1) * 512], s_hq)
            p.wait("vector", s_hq, base_hq + 9 * 16)
            for stg in range(6):
                step = 1 << stg
                for blk in range(0, 64, 2 * step):
                    for i in range(blk, blk + step):
                        j = i + step
                        p.emit("vector", lambda e, i=i, j=j: e.tensor_add(
                            hq_sb[:, i, :], hq_sb[:, i, :], hq_sb[:, j, :]))
                        p.emit("vector", lambda e, i=i, j=j: e.scalar_tensor_tensor(
                            hq_sb[:, j, :], hq_sb[:, j, :], -2.0, hq_sb[:, i, :],
                            ALU.mult, ALU.add))
            p.emit("vector", lambda e: e.engine_nop(), inc=s_bf)
            for ws in range(8):
                p.wait_now("gpsimd", s_cev)
                p.dma("gpsimd", w2_sb[:].rearrange("p a b -> p (a b)"),
                      w2p[ws].rearrange("p a b -> p (a b)"), s_w2)
                p.wait_now("tensor", s_w2)
                p.wait_now("tensor", s_bf)
                for dot in range(2):
                    bank = (2 * ws + dot) % 8
                    for ch in range(64):
                        p.emit("tensor", lambda e, dot=dot, ch=ch, bank=bank: e.matmul(
                            ps[:, bank], w2_sb[:, ch, dot * 128:(dot + 1) * 128],
                            hq_sb[:, ch, :], start=(ch == 0), stop=(ch == 63)),
                            inc=(s_cmm if ch == 63 else None))
                    p.wait_now("vector", s_cmm)
                    p.emit("vector", lambda e, ws=ws, dot=dot, bank=bank: e.tensor_copy(
                        yq_sb[:, ws * 2 + dot, :], ps[:, bank]), inc=s_cev)
            p.wait_now("vector", s_cev)
            for half in range(512 // LNW):
                off = lq * 512 + half * LNW
                hoff = half * LNW
                for d in range(16):
                    p.emit("vector", lambda e, d=d, hoff=hoff: e.tensor_add(
                        t_sb[:, d, 0:LNW], s1q_sb[:, d, hoff:hoff + LNW],
                        yq_sb[:, d, hoff:hoff + LNW]), inc=s_t)
                fm_ln(p, lambda d: t_sb[:, d, 0:LNW], 16, LNW, ps_sum, ps_sq,
                      ps_mu, ps_rstd, ones_sb, row_sb,
                      lambda d: sq_sb[:, d, 0:LNW], lambda d: None, D,
                      s_t, s_sq, s_st, s_row, s_rep, cons)
                p.emit("vector", lambda e: e.engine_nop(), inc=s_s1)
                p.wait_now("sync", s_s1)
                p.dma("sync", s2o[:, :, off:off + LNW], sq_sb[:, :, 0:LNW], s_out)
                p.wait_now("vector", s_out)

        for e in ENGINES:
            p.wait_now(e, s_out)
            p.wait_now(e, s_s1st)
            p.wait_now(e, s_h1st)

        with nc.Block() as blk:
            p.build(blk)
    return nc


_NCC = {}


def _get(name, builder):
    if name not in _NCC:
        _NCC[name] = builder()
    return _NCC[name]


def run_l0ffn(x_fm, y0_fm, W1, W2, W3):
    nc = _get("l0ffn", build_l0ffn)
    w1p = pack_lhsT(np.ascontiguousarray(W1.T), 512)
    w3p = pack_lhsT(np.ascontiguousarray(W3.T), 512)
    w2p = pack_lhsT(np.ascontiguousarray(W2.T), 256)
    hcon = hadamard(128).astype(np.float32)
    in_maps = []
    for c in range(N_CORES):
        sl = slice(c * TPC, (c + 1) * TPC)
        in_maps.append({
            "xs": pack_fm(np.ascontiguousarray(x_fm[:, sl])),
            "ys": pack_fm(np.ascontiguousarray(y0_fm[:, sl])),
            "w1p": w1p, "w3p": w3p, "w2p": w2p, "hcon": hcon,
        })
    res = _run(nc, in_maps, ["s2o"])
    out = np.empty((D, TOK), np.float32)
    for c in range(N_CORES):
        out[:, c * TPC:(c + 1) * TPC] = unpack_fm(res[c]["s2o"])
    return out



def build_mamba():
    from contextlib import ExitStack
    nc = bass.Bass("TRN2", target_bir_lowering=False, debug=False)
    NL = L
    xf = nc.dram_tensor("xf", [128, 16, NL], F32, kind="ExternalInput").ap()
    xoh = nc.dram_tensor("xoh", [128, 8, NL], F32, kind="ExternalInput").ap()
    wxp = nc.dram_tensor("wxp", [128, 16, 96], F32, kind="ExternalInput").ap()
    wdtp = nc.dram_tensor("wdtp", [64, 8, 128], F32, kind="ExternalInput").ap()
    bdtp = nc.dram_tensor("bdtp", [128, 8], F32, kind="ExternalInput").ap()
    acol = nc.dram_tensor("acol", [128, 8, 16], F32, kind="ExternalInput").ap()
    dcol = nc.dram_tensor("dcol", [128, 8], F32, kind="ExternalInput").ap()
    idcon = nc.dram_tensor("idcon", [128, 128], F32, kind="ExternalInput").ap()
    y0o = nc.dram_tensor("y0o", [128, 8, NL], F32, kind="ExternalOutput").ap()
    bcrep = nc.dram_tensor("bcrep", [32, 128, NL], F32, kind="Internal").ap()

    with ExitStack() as st:
        p = Prog(nc, st)
        sb = lambda name, shape, dt: st.enter_context(nc.sbuf_tensor(name, shape, dt))
        ones_sb = sb("ones", [128, 128], F32)
        id_sb = sb("idn", [128, 128], BF16)
        cons_sb = sb("cons", [128, 8], F32)
        wx_sb = sb("wx", [128, 16, 96], BF16)
        wdt_sb = sb("wdt", [64, 8, 128], BF16)
        bdt_sb = sb("bdt", [128, 8], F32)
        a_sb = sb("acl", [128, 8, 16], F32)
        d_sb = sb("dcl", [128, 8], F32)
        xt_sb = sb("xt", [128, 2, 512], BF16)
        xp_sb = sb("xp", [64, 8, 512], BF16)
        bc_sb = sb("bc", [32, NL], F32)
        brow_sb = sb("brow", [1, NL], F32)
        brep_sb = sb("brep", [128, NL], F32)
        crep_sb = sb("crep", [128, NL], F32)
        xd_sb = sb("xdt", [128, NL], BF16)
        dt_sb = sb("dt", [128, NL], F32)
        u_sb = sb("u", [128, NL], F32)
        da_sb = sb("da", [128, NL], F32)
        dbx_sb = sb("dbx", [128, NL], F32)
        hcb_sb = sb("hcb", [128, NL], BF16)
        sa_sb = sb("sa", [128, 512], F32)
        sb2_sb = sb("sb2", [128, 512], F32)
        ps = st.enter_context(nc.psum_tensor("ps", [128, 8, 512], F32))

        s_cst = p.sem("s_cst"); s_w = p.sem("s_w")
        s_xt0 = p.sem("s_xt0"); s_xt1 = p.sem("s_xt1")
        s_xp = p.sem("s_xp"); s_xpe = p.sem("s_xpe")
        s_br = p.sem("s_br"); s_bmm = p.sem("s_bmm"); s_bev = p.sem("s_bev")
        s_bst = p.sem("s_bst")
        s_xload = p.sem("s_xload"); s_dte = p.sem("s_dte")
        s_bcl = p.sem("s_bcl")
        s_dbx = p.sem("s_dbx"); s_scan = p.sem("s_scan"); s_hc = p.sem("s_hc2")
        s_ymm = p.sem("s_ymm"); s_ye = p.sem("s_ye"); s_yst = p.sem("s_yst")
        s_dtv = p.sem("s_dtv"); s_exp = p.sem("s_exp"); s_ccl = p.sem("s_ccl")

        p.emit("vector", lambda e: e.memset(ones_sb[:], 1.0))
        p.emit("vector", lambda e: e.memset(cons_sb[:, 0:1], -1.0))
        p.emit("vector", lambda e: e.memset(cons_sb[:, 1:2], 1.0), inc=s_cst)
        p.dma("gpsimd", id_sb[:], idcon[:], s_w)
        p.dma("gpsimd", wx_sb[:], wxp[:], s_w)
        p.dma("gpsimd", wdt_sb[:], wdtp[:], s_w)
        p.dma("gpsimd", bdt_sb[:], bdtp[:], s_w)
        p.dma("gpsimd", a_sb[:], acol[:], s_w)
        p.dma("gpsimd", d_sb[:], dcol[:], s_w)
        p.wait("tensor", s_w, 96)
        p.wait("tensor", s_cst, 1)
        p.wait("scalar", s_cst, 1)
        p.wait("scalar", s_w, 96)
        p.emit("scalar", lambda e: e.activation(
            a_sb[:].rearrange("p a b -> p (a b)"), a_sb[:].rearrange("p a b -> p (a b)"),
            AF.Exp), inc=s_cst)
        p.wait("vector", s_cst, 2)
        p.emit("vector", lambda e: e.tensor_scalar_mul(
            a_sb[:].rearrange("p a b -> p (a b)"), a_sb[:].rearrange("p a b -> p (a b)"),
            -1.0), inc=s_cst)
        p.wait("scalar", s_cst, 3)

        for lc in range(8):
            for dic in range(16):
                k = lc * 16 + dic
                if k >= 2:
                    p.wait("gpsimd", s_xp, k - 1)
                p.dma("gpsimd", xt_sb[:, k % 2], xf[:, dic, lc * 512:(lc + 1) * 512],
                      s_xt0 if k % 2 == 0 else s_xt1)
                p.wait("tensor", s_xt0 if k % 2 == 0 else s_xt1, 16 * (k // 2 + 1))
                if dic == 0 and lc >= 1:
                    p.wait("tensor", s_xpe, lc)
                p.emit("tensor", lambda e, dic=dic, k=k: e.matmul(
                    ps[0:96, 0, :], wx_sb[:, dic, :], xt_sb[:, k % 2],
                    start=(dic == 0), stop=(dic == 15)), inc=s_xp)
            p.wait("vector", s_xp, p.cnt[s_xp])
            p.emit("vector", lambda e, lc=lc: e.tensor_copy(xp_sb[:, lc, :], ps[0:64, 0, :]))
            p.emit("vector", lambda e, lc=lc: e.tensor_copy(
                bc_sb[0:32, lc * 512:(lc + 1) * 512], ps[64:96, 0, :]), inc=s_xpe)

        p.wait_now("gpsimd", s_xpe)
        for r in range(32):
            if r >= 1:
                p.wait("gpsimd", s_bmm, 8 * r)
            p.dma("gpsimd", brow_sb[0:1, :], bc_sb[r:r + 1, :], s_br)
            p.wait("tensor", s_br, 16 * (r + 1))
            if r >= 1:
                p.wait("tensor", s_bev, 8 * r)
            for ch in range(8):
                p.emit("tensor", lambda e, ch=ch: e.matmul(
                    ps[:, ch, :], ones_sb[0:1, :], brow_sb[0:1, ch * 512:(ch + 1) * 512],
                    start=True, stop=True), inc=s_bmm)
            p.wait("vector", s_bmm, p.cnt[s_bmm])
            if r >= 1:
                p.wait("vector", s_bst, 16 * r)
            for ch in range(8):
                p.emit("vector", lambda e, ch=ch: e.tensor_copy(
                    brep_sb[:, ch * 512:(ch + 1) * 512], ps[:, ch, :]), inc=s_bev)
            p.wait("sync", s_bev, p.cnt[s_bev])
            p.dma("sync", bcrep[r], brep_sb[:], s_bst)
        p.barrier()
        p.wait_now("gpsimd", s_bst)

        for j in range(8):
            idx0 = 16 * j
            if j >= 1:
                p.wait("gpsimd", s_ye, 8 * j)
            p.dma("gpsimd", xd_sb[:], xoh[:, j, :], s_xload)
            for lc in range(8):
                if j >= 1 or lc >= 1:
                    p.wait("tensor", s_dtv, 2 * (8 * j + lc))
                if j >= 1 and lc == 0:
                    p.wait("tensor", s_ye, 8 * j)
                p.emit("tensor", lambda e, j=j, lc=lc: e.matmul(
                    ps[:, 7, :], wdt_sb[:, j, :], xp_sb[:, lc, :],
                    start=True, stop=True), inc=s_dte)
                p.wait("scalar", s_dte, p.cnt[s_dte])
                p.emit("scalar", lambda e, j=j: e.activation(
                    sa_sb[:], ps[:, 7, :], AF.Abs, bias=bdt_sb[:, j:j + 1]))
                p.emit("scalar", lambda e: e.activation(
                    sa_sb[:], sa_sb[:], AF.Exp, scale=cons_sb[:, 0:1]))
                p.emit("scalar", lambda e: e.activation(
                    sa_sb[:], sa_sb[:], AF.Ln, bias=cons_sb[:, 1:2]), inc=s_dte)
                p.wait("vector", s_dte, p.cnt[s_dte])
                p.wait("vector", s_xload, 16 * (j + 1))
                if j >= 1 and lc == 0:
                    p.wait("vector", s_exp, 16 * j)
                p.emit("vector", lambda e, j=j: e.tensor_scalar(
                    sb2_sb[:], ps[:, 7, :], bdt_sb[:, j:j + 1], 0.0, ALU.add, ALU.max),
                    inc=s_dtv)
                p.emit("vector", lambda e, lc=lc: e.tensor_add(
                    dt_sb[:, lc * 512:(lc + 1) * 512], sb2_sb[:], sa_sb[:]))
                p.emit("vector", lambda e, lc=lc: e.tensor_mul(
                    u_sb[:, lc * 512:(lc + 1) * 512],
                    dt_sb[:, lc * 512:(lc + 1) * 512],
                    xd_sb[:, lc * 512:(lc + 1) * 512]), inc=s_dtv)
            for n in range(16):
                idx = idx0 + n
                p.wait("gpsimd", s_dbx, idx)
                p.dma("gpsimd", brep_sb[:], bcrep[n], s_bcl)
                p.wait("gpsimd", s_hc, idx)
                p.dma("gpsimd", crep_sb[:], bcrep[16 + n], s_ccl)
                if n == 0:
                    p.wait("scalar", s_yst, 16 * j)
                    p.wait("scalar", s_dtv, 16 * (j + 1))
                p.wait("scalar", s_scan, idx)
                p.wait("scalar", s_hc, idx)
                p.emit("scalar", lambda e, j=j, n=n: e.activation(
                    da_sb[:], dt_sb[:], AF.Exp, scale=a_sb[:, j, n:n + 1]), inc=s_exp)
                p.wait("vector", s_bcl, 16 * (idx + 1))
                p.wait("vector", s_ymm, idx)
                p.emit("vector", lambda e: e.tensor_mul(dbx_sb[:], u_sb[:], brep_sb[:]),
                       inc=s_dbx)
                p.wait("vector", s_exp, p.cnt[s_exp])
                p.emit("vector", lambda e: e.tensor_tensor_scan(
                    da_sb[:], da_sb[:], dbx_sb[:], 0.0, ALU.mult, ALU.add), inc=s_scan)
                p.wait("vector", s_ccl, 16 * (idx + 1))
                p.emit("vector", lambda e: e.tensor_mul(hcb_sb[:], da_sb[:], crep_sb[:]),
                       inc=s_hc)
                p.wait("tensor", s_hc, p.cnt[s_hc])
                if n == 0 and j >= 1:
                    p.wait("tensor", s_ye, 8 * j)
                for lch in range(8):
                    p.emit("tensor", lambda e, lch=lch, n=n: e.matmul(
                        ps[:, lch, :], id_sb[:], hcb_sb[:, lch * 512:(lch + 1) * 512],
                        start=(n == 0), stop=(n == 15)),
                        inc=(s_ymm if lch == 7 else None))
            p.wait("vector", s_ymm, p.cnt[s_ymm])
            p.emit("vector", lambda e, j=j: e.tensor_scalar(
                u_sb[:], xd_sb[:], d_sb[:, j:j + 1], None, ALU.mult, ALU.bypass))
            for lch in range(8):
                p.emit("vector", lambda e, lch=lch: e.tensor_add(
                    da_sb[:, lch * 512:(lch + 1) * 512], ps[:, lch, :],
                    u_sb[:, lch * 512:(lch + 1) * 512]), inc=s_ye)
            p.wait("sync", s_ye, p.cnt[s_ye])
            if j >= 1:
                p.wait("sync", s_yst, 16 * j)
            p.dma("sync", y0o[:, j, :], da_sb[:], s_yst)

        for e in ENGINES:
            p.wait_now(e, s_yst)
            p.wait_now(e, s_bst)
        with nc.Block() as blk:
            p.build(blk)
    return nc


def run_mamba(x, Wx, Wdt, bdt, A_log, Dvec):
    nc = _get("mamba", build_mamba)
    wxp = pack_fm(np.ascontiguousarray(Wx.T))
    WdtT = np.ascontiguousarray(Wdt.T)
    in_maps = []
    for c in range(N_CORES):
        b, dh = c // 2, c % 2
        dsl = slice(dh * 1024, (dh + 1) * 1024)
        xT = np.ascontiguousarray(x[b].T)
        wdtp = np.ascontiguousarray(
            WdtT[:, dsl].reshape(64, 8, 128))
        in_maps.append({
            "xf": pack_fm(xT),
            "xoh": pack_fm(np.ascontiguousarray(xT[dsl])),
            "wxp": wxp,
            "wdtp": wdtp,
            "bdtp": np.ascontiguousarray(bdt[dsl].reshape(8, 128).T),
            "acol": np.ascontiguousarray(A_log[dsl].reshape(8, 128, 16).transpose(1, 0, 2)),
            "dcol": np.ascontiguousarray(Dvec[dsl].reshape(8, 128).T),
            "idcon": np.eye(128, dtype=np.float32),
        })
    res = _run(nc, in_maps, ["y0o"])
    y0 = np.empty((D, TOK), np.float32)
    for c in range(N_CORES):
        b, dh = c // 2, c % 2
        y0[dh * 1024:(dh + 1) * 1024, b * L:(b + 1) * L] = unpack_fm(res[c]["y0o"])
    return y0



def build_l4a():
    from contextlib import ExitStack
    nc = bass.Bass("TRN2", target_bir_lowering=False, debug=False)
    s2i = nc.dram_tensor("s2i", [128, 16, TPC], F32, kind="ExternalInput").ap()
    pai = nc.dram_tensor("pai", [128, 16, TPC], BF16, kind="ExternalInput").ap()
    pbi = nc.dram_tensor("pbi", [128, 16, TPC], BF16, kind="ExternalInput").ap()
    g1i = nc.dram_tensor("g1i", [128, 16], F32, kind="ExternalInput").ap()
    b1i = nc.dram_tensor("b1i", [128, 16], F32, kind="ExternalInput").ap()
    s3o = nc.dram_tensor("s3o", [128, 16, TPC], F32, kind="ExternalOutput").ap()
    xno = nc.dram_tensor("xno", [128, 16, TPC], BF16, kind="ExternalOutput").ap()

    LNW = 512
    NLN = TPC // LNW
    with ExitStack() as st:
        p = Prog(nc, st)
        sb = lambda name, shape, dt: st.enter_context(nc.sbuf_tensor(name, shape, dt))
        ones_sb = sb("ones", [128, 128], F32)
        cons_sb = sb("cons", [1, 8], F32)
        cons = {"eps": cons_sb[0:1, 0:1], "neghalf": cons_sb[0:1, 1:2]}
        row_sb = sb("row", [1, 6, LNW], F32)
        g1_sb = sb("g1", [128, 16], F32)
        b1_sb = sb("b1", [128, 16], F32)
        t_sb = sb("t", [128, 16, LNW], F32)
        sq_sb = sb("sq", [128, 16, LNW], F32)
        s3_sb = sb("s3", [128, 16, LNW], F32)
        xbuf = sb("xb", [128, 3, LNW], F32)
        pabuf = sb("pa", [128, 3, LNW], BF16)
        pbbuf = sb("pb", [128, 3, LNW], BF16)
        ps = st.enter_context(nc.psum_tensor("ps", [128, 8, 512], F32))
        ps_sum = ps[0:1, 6, 0:LNW]; ps_sq = ps[0:1, 5, 0:LNW]
        ps_mu = ps[:, 7, 0:LNW]; ps_rstd = ps[:, 4, 0:LNW]

        s_ld = p.sem("s_ld"); s_t = p.sem("s_t"); s_sq = p.sem("s_sq")
        s_st = p.sem("s_st"); s_row = p.sem("s_row"); s_rep = p.sem("s_rep")
        s_o = p.sem("s_o"); s_ost = p.sem("s_ost"); s_w = p.sem("s_w")
        s_ox = p.sem("s_ox")

        p.emit("vector", lambda e: e.memset(ones_sb[:], 1.0))
        p.emit("vector", lambda e: e.memset(cons_sb[0:1, 0:1], float(EPS)))
        p.emit("vector", lambda e: e.memset(cons_sb[0:1, 1:2], -0.5), inc=s_t)
        p.dma("gpsimd", g1_sb[:], g1i[:], s_w)
        p.dma("gpsimd", b1_sb[:], b1i[:], s_w)
        p.wait("tensor", s_t, 1)
        p.wait("scalar", s_t, 1)
        p.wait("vector", s_w, 32)

        for lc in range(NLN):
            for dic in range(16):
                k = lc * 16 + dic
                p.wait_now("gpsimd", s_t)
                p.dma("gpsimd", xbuf[:, dic % 3], s2i[:, dic, lc * LNW:(lc + 1) * LNW], s_ld)
                p.dma("gpsimd", pabuf[:, dic % 3], pai[:, dic, lc * LNW:(lc + 1) * LNW], s_ld)
                p.dma("gpsimd", pbbuf[:, dic % 3], pbi[:, dic, lc * LNW:(lc + 1) * LNW], s_ld)
                p.wait_now("vector", s_ld)
                p.emit("vector", lambda e, d=dic: e.tensor_add(
                    sq_sb[:, d], pabuf[:, d % 3], pbbuf[:, d % 3]))
                p.emit("vector", lambda e, d=dic: e.tensor_add(
                    t_sb[:, d], sq_sb[:, d], xbuf[:, d % 3]), inc=s_t)

            def apply1(d, lc=lc):
                p.emit("vector", lambda e, d=d: e.tensor_copy(s3_sb[:, d], sq_sb[:, d]),
                       inc=s_o)

            fm_ln(p, lambda d: t_sb[:, d], 16, LNW, ps_sum, ps_sq, ps_mu, ps_rstd,
                  ones_sb, row_sb, lambda d: sq_sb[:, d], apply1, D,
                  s_t, s_sq, s_st, s_row, s_rep, cons)
            p.wait_now("sync", s_o)
            p.dma("sync", s3o[:, :, lc * LNW:(lc + 1) * LNW], s3_sb[:], s_ost)
            for dic in range(16):
                p.emit("vector", lambda e, d=dic: e.tensor_copy(t_sb[:, d], s3_sb[:, d]),
                       inc=s_t)

            def apply2(d, lc=lc):
                p.emit("vector", lambda e, d=d: e.tensor_scalar(
                    sq_sb[:, d], sq_sb[:, d], g1_sb[:, d:d + 1], None, ALU.mult, ALU.bypass))
                p.emit("vector", lambda e, d=d: e.tensor_scalar(
                    sq_sb[:, d], sq_sb[:, d], b1_sb[:, d:d + 1], None, ALU.add, ALU.bypass),
                    inc=s_o)

            fm_ln(p, lambda d: t_sb[:, d], 16, LNW, ps_sum, ps_sq, ps_mu, ps_rstd,
                  ones_sb, row_sb, lambda d: sq_sb[:, d], apply2, D,
                  s_t, s_sq, s_st, s_row, s_rep, cons)
            p.wait_now("gpsimd", s_o)
            p.dma("gpsimd", xno[:, :, lc * LNW:(lc + 1) * LNW],
                  sq_sb[:, :, 0:LNW], s_ox)
            p.wait_now("vector", s_ost)
            p.wait_now("vector", s_ox)

        for e in ENGINES:
            p.wait_now(e, s_ost)
            p.wait_now(e, s_ox)
        with nc.Block() as blk:
            p.build(blk)
    return nc



def build_l4c():
    from contextlib import ExitStack
    nc = bass.Bass("TRN2", target_bir_lowering=False, debug=False)
    s3i = nc.dram_tensor("s3i", [128, 16, TPC], F32, kind="ExternalInput").ap()
    pai = nc.dram_tensor("pai", [128, 16, TPC], BF16, kind="ExternalInput").ap()
    pbi = nc.dram_tensor("pbi", [128, 16, TPC], BF16, kind="ExternalInput").ap()
    f1p = nc.dram_tensor("f1p", [16, 128, 16, 512], F32, kind="ExternalInput").ap()
    f2p = nc.dram_tensor("f2p", [8, 128, 64, 256], F32, kind="ExternalInput").ap()
    g2i = nc.dram_tensor("g2i", [128, 16], F32, kind="ExternalInput").ap()
    b2i = nc.dram_tensor("b2i", [128, 16], F32, kind="ExternalInput").ap()
    sbi = nc.dram_tensor("sbi", [128, 16], F32, kind="ExternalInput").ap()
    wdp = nc.dram_tensor("wdp", [128, 16, 32], F32, kind="ExternalInput").ap()
    wup = nc.dram_tensor("wup", [32, 16, 128], F32, kind="ExternalInput").ap()
    wrt = nc.dram_tensor("wrt", [128, 16, 3], F32, kind="ExternalInput").ap()
    s4o = nc.dram_tensor("s4o", [128, 16, TPC], F32, kind="ExternalOutput").ap()
    hid = nc.dram_tensor("hid", [64, 128, TPC], BF16, kind="Internal").ap()
    xps = nc.dram_tensor("xps", [128, 16, TPC], F32, kind="Internal").ap()

    LNW = 256
    NLC = TPC // 512
    with ExitStack() as st:
        p = Prog(nc, st)
        sb = lambda name, shape, dt: st.enter_context(nc.sbuf_tensor(name, shape, dt))
        ones_sb = sb("ones", [128, 128], F32)
        onesb_sb = sb("onesb", [128, 128], BF16)
        cons_sb = sb("cons", [1, 8], F32)
        cons = {"eps": cons_sb[0:1, 0:1], "neghalf": cons_sb[0:1, 1:2]}
        row_sb = sb("row", [1, 6, LNW], F32)
        g2_sb = sb("g2", [128, 16], F32)
        b2_sb = sb("b2", [128, 16], F32)
        sb_sb = sb("sbv", [128, 16], F32)
        wd_sb = sb("wd", [128, 16, 32], BF16)
        wu_sb = sb("wu", [32, 16, 128], BF16)
        wr_sb = sb("wr", [128, 16, 3], BF16)
        mid_sb = sb("mid", [32, TPC], BF16)
        lrow_sb = sb("lrow", [1, 3, LNW], F32)
        mrow_sb = sb("mrow", [1, 3, TPC], BF16)
        mrep_sb = sb("mrep", [128, 3, LNW], F32)
        t_sb = sb("t", [128, 16, LNW], F32)
        sq_sb = sb("sq", [128, 16, LNW], F32)
        s3c_sb = sb("s3c", [128, 16, LNW], F32)
        xbuf = sb("xb", [128, 3, LNW], F32)
        x2buf = sb("x2", [128, 3, LNW], F32)
        pabuf = sb("pa", [128, 3, LNW], BF16)
        pbbuf = sb("pb", [128, 3, LNW], BF16)
        pool = sb("pool", [128, 57344], BF16)
        xn2_sb = pool[:, 0:32768].rearrange("p (a b) -> p a b", a=16)
        wf_sb = pool[:, 32768:40960].rearrange("p (a b) -> p a b", a=16)
        h1_sb = pool[:, 40960:57344].rearrange("p (a b c) -> p a b c", a=2, b=4)
        hq_sb = pool[:, 0:32768].rearrange("p (a b) -> p a b", a=64)
        w2_sb = pool[:, 32768:49152].rearrange("p (a b) -> p a b", a=64)
        yq_sb = pool[:, 49152:57344].rearrange("p (a b) -> p a b", a=16)
        ps = st.enter_context(nc.psum_tensor("ps", [128, 8, 512], F32))
        ps_sum = ps[0:1, 6, 0:LNW]; ps_sq = ps[0:1, 5, 0:LNW]
        ps_mu = ps[:, 7, 0:LNW]; ps_rstd = ps[:, 4, 0:LNW]

        s_ld = p.sem("s_ld"); s_t = p.sem("s_t"); s_sq = p.sem("s_sq")
        s_st = p.sem("s_st"); s_row = p.sem("s_row"); s_rep = p.sem("s_rep")
        s_o = p.sem("s_o"); s_w = p.sem("s_w")
        s_lmm = p.sem("s_lmm"); s_lev = p.sem("s_lev")
        s_wf = p.sem("s_wf"); s_amm = p.sem("s_amm"); s_aev = p.sem("s_aev")
        s_hst = p.sem("s_hst"); s_xst = p.sem("s_xst")
        s_hq = p.sem("s_hq"); s_w2 = p.sem("s_w2"); s_cmm = p.sem("s_cmm")
        s_cev = p.sem("s_cev"); s_um = p.sem("s_um"); s_ue = p.sem("s_ue")
        s_mm2 = p.sem("s_mm2"); s_me = p.sem("s_me"); s_fin = p.sem("s_fin")
        s_x2 = p.sem("s_x2"); s_lx2 = p.sem("s_lx2")

        p.emit("vector", lambda e: e.memset(ones_sb[:], 1.0))
        p.emit("vector", lambda e: e.memset(onesb_sb[:], 1.0))
        p.emit("vector", lambda e: e.memset(cons_sb[0:1, 0:1], float(EPS)))
        p.emit("vector", lambda e: e.memset(cons_sb[0:1, 1:2], -0.5), inc=s_t)
        for dst, srcn in ((g2_sb, g2i), (b2_sb, b2i), (sb_sb, sbi), (wd_sb, wdp),
                          (wu_sb, wup), (wr_sb, wrt)):
            p.dma("gpsimd", dst[:], srcn[:], s_w)
        p.wait("tensor", s_t, 1)
        p.wait("scalar", s_t, 1)
        p.wait("vector", s_w, 96)
        p.wait("tensor", s_w, 96)

        for lc in range(NLC):
            for half in range(2):
                off = lc * 512 + half * 256
                for dic in range(16):
                    p.wait_now("gpsimd", s_t)
                    p.dma("gpsimd", xbuf[:, dic % 3], s3i[:, dic, off:off + 256], s_ld)
                    p.dma("gpsimd", pabuf[:, dic % 3], pai[:, dic, off:off + 256], s_ld)
                    p.dma("gpsimd", pbbuf[:, dic % 3], pbi[:, dic, off:off + 256], s_ld)
                    p.wait_now("vector", s_ld)
                    p.emit("vector", lambda e, d=dic: e.tensor_add(
                        sq_sb[:, d], pabuf[:, d % 3], pbbuf[:, d % 3]))
                    p.emit("vector", lambda e, d=dic: e.tensor_add(
                        t_sb[:, d], sq_sb[:, d], xbuf[:, d % 3]))
                    p.emit("vector", lambda e, d=dic, off=off: e.tensor_copy(
                        xn2_sb[:, d, off:off + 256], xbuf[:, d % 3]), inc=s_t)
                p.wait_now("tensor", s_t)
                if lc + half > 0:
                    p.wait("tensor", s_lev, p.cnt[s_lev])
                for dic in range(16):
                    for cch in range(3):
                        p.emit("tensor", lambda e, d=dic, c=cch, off=off: e.matmul(
                            ps[0:1, c, 0:256], wr_sb[:, d, c:c + 1],
                            xn2_sb[:, d, off:off + 256],
                            start=(d == 0), stop=(d == 15)))
                    p.emit("tensor", lambda e, d=dic, off=off: e.matmul(
                        ps[0:32, 3, 0:256], wd_sb[:, d, :], xn2_sb[:, d, off:off + 256],
                        start=(d == 0), stop=(d == 15)),
                        inc=(s_lmm if dic == 15 else None))
                p.wait_now("vector", s_lmm)
                for cch in range(3):
                    p.emit("vector", lambda e, c=cch: e.tensor_copy(
                        lrow_sb[0:1, c, :], ps[0:1, c, 0:256]))
                p.emit("vector", lambda e, off=off: e.tensor_copy(
                    mid_sb[0:32, off:off + 256], ps[0:32, 3, 0:256]), inc=s_lev)
                lr = lambda c: lrow_sb[0:1, c, :]
                p.emit("vector", lambda e: e.tensor_tensor(row_sb[0:1, 0, :], lr(0), lr(1), ALU.is_ge))
                p.emit("vector", lambda e: e.tensor_tensor(row_sb[0:1, 1, :], lr(0), lr(2), ALU.is_ge))
                p.emit("vector", lambda e, off=off: e.tensor_mul(
                    mrow_sb[0:1, 0, off:off + 256], row_sb[0:1, 0, :], row_sb[0:1, 1, :]))
                p.emit("vector", lambda e: e.tensor_tensor(row_sb[0:1, 2, :], lr(1), lr(0), ALU.is_gt))
                p.emit("vector", lambda e: e.tensor_tensor(row_sb[0:1, 3, :], lr(1), lr(2), ALU.is_ge))
                p.emit("vector", lambda e, off=off: e.tensor_mul(
                    mrow_sb[0:1, 1, off:off + 256], row_sb[0:1, 2, :], row_sb[0:1, 3, :]))
                p.emit("vector", lambda e, off=off: e.tensor_add(
                    row_sb[0:1, 4, :], mrow_sb[0:1, 0, off:off + 256],
                    mrow_sb[0:1, 1, off:off + 256]))
                p.emit("vector", lambda e, off=off: e.tensor_scalar(
                    mrow_sb[0:1, 2, off:off + 256], row_sb[0:1, 4, :], -1.0, 1.0,
                    ALU.mult, ALU.add), inc=s_t)
                p.wait_now("sync", s_t)
                p.dma("sync", xps[:, :, off:off + 256], t_sb[:, :, 0:LNW], s_xst)

                def applyx(d, off=off):
                    p.emit("vector", lambda e, d=d: e.tensor_scalar(
                        sq_sb[:, d], sq_sb[:, d], g2_sb[:, d:d + 1], None, ALU.mult, ALU.bypass))
                    p.emit("vector", lambda e, d=d, off=off: e.tensor_scalar(
                        xn2_sb[:, d, off:off + 256], sq_sb[:, d], b2_sb[:, d:d + 1], None,
                        ALU.add, ALU.bypass), inc=s_o)

                fm_ln(p, lambda d: t_sb[:, d], 16, LNW, ps_sum, ps_sq, ps_mu, ps_rstd,
                      ones_sb, row_sb, lambda d: sq_sb[:, d], applyx, D,
                      s_t, s_sq, s_st, s_row, s_rep, cons)
                p.wait_now("vector", s_xst)
        p.barrier()

        for s in range(16):
            if s >= 1:
                p.wait("gpsimd", s_amm, 16 * s)
            p.dma("gpsimd", wf_sb[:].rearrange("p a b -> p (a b)"),
                  f1p[s].rearrange("p a b -> p (a b)"), s_wf)
        kk = 0
        for s in range(16):
            p.wait("tensor", s_wf, 16 * (s + 1))
            for ht in range(4):
                for lc in range(4):
                    bank = kk % 8
                    if kk >= 8:
                        p.wait("tensor", s_aev, kk - 7)
                    for d in range(16):
                        p.emit("tensor", lambda e, ht=ht, lc=lc, d=d, bank=bank: e.matmul(
                            ps[:, bank], wf_sb[:, d, ht * 128:(ht + 1) * 128],
                            xn2_sb[:, d, lc * 512:(lc + 1) * 512],
                            start=(d == 0), stop=(d == 15)),
                            inc=(s_amm if d == 15 else None))
                    p.wait("scalar", s_amm, kk + 1)
                    if s >= 2 and ht == 0 and lc == 0:
                        p.wait("scalar", s_hst, 64 * (s - 1))
                    p.emit("scalar", lambda e, s=s, ht=ht, lc=lc, bank=bank: e.activation(
                        h1_sb[:, s % 2, ht, lc * 512:(lc + 1) * 512], ps[:, bank],
                        AF.Gelu_apprx_tanh), inc=s_aev)
                    kk += 1
            p.wait("sync", s_aev, 16 * (s + 1))
            p.wait("sync", s_hst, 64 * s)
            for ht in range(4):
                p.dma("sync", hid[4 * s + ht], h1_sb[:, s % 2, ht], s_hst)
        p.barrier()

        p.wait_now("gpsimd", s_hst)
        p.wait_now("gpsimd", s_xst)
        for lq in range(NLC):
            if lq >= 1:
                p.wait("gpsimd", s_fin, 32 * lq)
            base_hq = p.cnt[s_hq]
            for grp in range(8):
                p.dma("gpsimd", hq_sb[:, grp * 8:(grp + 1) * 8, :],
                      hid[grp * 8:(grp + 1) * 8, :, lq * 512:(lq + 1) * 512]
                      .rearrange("a p b -> p a b"), s_hq)
            for ws in range(8):
                p.wait_now("gpsimd", s_cev)
                p.dma("gpsimd", w2_sb[:].rearrange("p a b -> p (a b)"),
                      f2p[ws].rearrange("p a b -> p (a b)"), s_w2)
                p.wait_now("tensor", s_w2)
                p.wait("tensor", s_hq, base_hq + 8 * 16)
                for dot in range(2):
                    bank = (2 * ws + dot) % 8
                    for ch in range(64):
                        p.emit("tensor", lambda e, dot=dot, ch=ch, bank=bank: e.matmul(
                            ps[:, bank], w2_sb[:, ch, dot * 128:(dot + 1) * 128],
                            hq_sb[:, ch, :], start=(ch == 0), stop=(ch == 63)),
                            inc=(s_cmm if ch == 63 else None))
                    p.wait_now("vector", s_cmm)
                    p.emit("vector", lambda e, ws=ws, dot=dot, bank=bank: e.tensor_copy(
                        yq_sb[:, ws * 2 + dot, :], ps[:, bank]), inc=s_cev)
            for half in range(2):
                off = lq * 512 + half * 256
                hoff = half * 256
                for dic in range(16):
                    p.wait_now("gpsimd", s_t)
                    p.dma("gpsimd", s3c_sb[:, dic], s3i[:, dic, off:off + 256], s_ld)
                p.wait_now("vector", s_ld)
                p.wait_now("tensor", s_t)
                p.wait_now("tensor", s_cev)
                p.wait_now("tensor", s_o)
                for c in range(3):
                    p.emit("tensor", lambda e, c=c, off=off: e.matmul(
                        ps[:, 4 + c, 0:256], onesb_sb[0:1, :],
                        mrow_sb[0:1, c, off:off + 256], start=True, stop=True),
                        inc=s_mm2)
                p.wait_now("vector", s_mm2)
                for c in range(3):
                    p.emit("vector", lambda e, c=c: e.tensor_copy(
                        mrep_sb[:, c, :], ps[:, 4 + c, 0:256]), inc=s_me)
                for dic in range(16):
                    p.wait("gpsimd", s_x2, p.cnt[s_x2])
                    p.dma("gpsimd", x2buf[:, dic % 3], xps[:, dic, off:off + 256], s_lx2)
                    bank = dic % 2
                    p.wait("tensor", s_me, p.cnt[s_me])
                    p.wait("tensor", s_ue, max(0, p.cnt[s_um] - 1))
                    p.emit("tensor", lambda e, d=dic, off=off, bank=bank: e.matmul(
                        ps[:, bank, 0:256], wu_sb[:, d, :], mid_sb[0:32, off:off + 256],
                        start=True, stop=True), inc=s_um)
                    p.wait("vector", s_um, p.cnt[s_um])
                    p.wait_now("vector", s_lx2)
                    p.emit("vector", lambda e, d=dic: e.tensor_scalar(
                        sq_sb[:, d], s3c_sb[:, d], sb_sb[:, d:d + 1], None, ALU.add, ALU.bypass))
                    p.emit("vector", lambda e, d=dic: e.tensor_mul(
                        t_sb[:, d], sq_sb[:, d], mrep_sb[:, 0, :]))
                    p.emit("vector", lambda e, d=dic, bank=bank: e.tensor_add(
                        sq_sb[:, d], s3c_sb[:, d], ps[:, bank, 0:256]), inc=s_ue)
                    p.emit("vector", lambda e, d=dic: e.tensor_mul(
                        sq_sb[:, d], sq_sb[:, d], mrep_sb[:, 1, :]))
                    p.emit("vector", lambda e, d=dic: e.tensor_add(
                        t_sb[:, d], t_sb[:, d], sq_sb[:, d]))
                    p.emit("vector", lambda e, d=dic, hoff=hoff: e.tensor_add(
                        sq_sb[:, d], x2buf[:, d % 3], yq_sb[:, d, hoff:hoff + 256]),
                        inc=s_x2)
                    p.emit("vector", lambda e, d=dic: e.tensor_mul(
                        sq_sb[:, d], sq_sb[:, d], mrep_sb[:, 2, :]))
                    p.emit("vector", lambda e, d=dic: e.tensor_add(
                        t_sb[:, d], t_sb[:, d], sq_sb[:, d]))
                    p.emit("vector", lambda e, d=dic: e.tensor_add(
                        t_sb[:, d], t_sb[:, d], s3c_sb[:, d]), inc=s_t)

                fm_ln(p, lambda d: t_sb[:, d], 16, LNW, ps_sum, ps_sq, ps_mu, ps_rstd,
                      ones_sb, row_sb, lambda d: sq_sb[:, d],
                      lambda d: p.emit("vector", lambda e: e.engine_nop(), inc=s_o), D,
                      s_t, s_sq, s_st, s_row, s_rep, cons)
                p.wait_now("sync", s_o)
                p.dma("sync", s4o[:, :, off:off + 256], sq_sb[:, :, 0:LNW], s_fin)
                p.wait_now("vector", s_fin)

        for e in ENGINES:
            p.wait_now(e, s_fin)
        with nc.Block() as blk:
            p.build(blk)
    return nc



def build_l4b():
    from contextlib import ExitStack
    nc = bass.Bass("TRN2", target_bir_lowering=False, debug=False)
    NLL = L
    xni = nc.dram_tensor("xni", [128, 16, NLL], BF16, kind="ExternalInput").ap()
    wqp = nc.dram_tensor("wqp", [2, 128, 16, 512], F32, kind="ExternalInput").ap()
    wkp = nc.dram_tensor("wkp", [2, 128, 16, 512], F32, kind="ExternalInput").ap()
    wvp = nc.dram_tensor("wvp", [2, 128, 16, 512], F32, kind="ExternalInput").ap()
    wop = nc.dram_tensor("wop", [4, 128, 8, 512], F32, kind="ExternalInput").ap()
    mko = nc.dram_tensor("mko", [128, 128], F32, kind="ExternalInput").ap()
    mkt = nc.dram_tensor("mkt", [128, 128], F32, kind="ExternalInput").ap()
    idf = nc.dram_tensor("idf", [128, 128], F32, kind="ExternalInput").ap()
    po = nc.dram_tensor("po", [128, 16, NLL], BF16, kind="ExternalOutput").ap()
    qs = nc.dram_tensor("qs", [8, 128, NLL], BF16, kind="Internal").ap()
    ks = nc.dram_tensor("ks", [8, 128, NLL], BF16, kind="Internal").ap()
    vts = nc.dram_tensor("vts", [32, 128, 1024], BF16, kind="Internal").ap()
    ISCALE = float(1.0 / np.sqrt(128.0))

    with ExitStack() as st:
        p = Prog(nc, st)
        sb = lambda name, shape, dt: st.enter_context(nc.sbuf_tensor(name, shape, dt))
        onesb_sb = sb("onesb", [128, 128], BF16)
        onesf_sb = sb("onesf", [128, 128], F32)
        mko_sb = sb("mkoS", [128, 128], F32)
        mkt_sb = sb("mktS", [128, 128], F32)
        idf_sb = sb("idfS", [128, 128], F32)
        ctx_sb = sb("ctx", [128, 8, 8, 512], BF16)
        mx_sb = sb("mx", [128, 12], F32)
        dtmp_sb = sb("dtmp", [128, 128], F32)
        neg_sb = sb("negm", [128, 4], F32)
        srow_sb = sb("srow", [1, 4, 512], F32)
        rrep_sb = sb("rrep", [128, 512], F32)
        stt_sb = sb("sct", [128, 512], F32)
        pool = sb("pool", [128, 40960], BF16)
        wq_sb = pool[:, 0:8192].rearrange("p (a b) -> p a b", a=16)
        wk_sb = pool[:, 8192:16384].rearrange("p (a b) -> p a b", a=16)
        wv_sb = pool[:, 16384:24576].rearrange("p (a b) -> p a b", a=16)
        xn_sb = pool[:, 24576:32768].rearrange("p (a b) -> p a b", a=16)
        st1_sb = pool[:, 32768:37376].rearrange("p (a b) -> p a b", a=9)
        k_sb = pool[:, 0:4096]
        q_sb = pool[:, 4096:8192]
        vt_sb = pool[:, 8192:12288].rearrange("p (a b) -> p a b", a=32)
        pT_sb = pool[:, 16384:32768].rearrange("p (a b) -> p a b", a=32)
        wo_sb = pool[:, 0:16384].rearrange("p (a b c) -> p a b c", a=4, b=8)
        po_sb = pool[:, 16384:24576].rearrange("p (a b) -> p a b", a=16)
        ps = st.enter_context(nc.psum_tensor("ps", [128, 8, 512], F32))

        s_w = p.sem("s_w"); s_x = p.sem("s_x"); s_pm = p.sem("s_pm")
        s_pe = p.sem("s_pe"); s_pst = p.sem("s_pst")
        s_kl = p.sem("s_kl"); s_smm = p.sem("s_smm"); s_sst = p.sem("s_sst")
        s_tmm = p.sem("s_tmm"); s_tev = p.sem("s_tev"); s_pT = p.sem("s_pT")
        s_cmm = p.sem("s_cmm"); s_sum = p.sem("s_sum"); s_nrm = p.sem("s_nrm")
        s_omm = p.sem("s_omm"); s_oev = p.sem("s_oev"); s_ost = p.sem("s_ost")
        s_wo = p.sem("s_wo"); s_c0 = p.sem("s_c0"); s_wp = p.sem("s_wp")

        p.emit("vector", lambda e: e.memset(onesb_sb[:], 1.0))
        p.emit("vector", lambda e: e.memset(onesf_sb[:], 1.0), inc=s_c0)
        p.dma("gpsimd", mko_sb[:], mko[:], s_w)
        p.dma("gpsimd", mkt_sb[:], mkt[:], s_w)
        p.dma("gpsimd", idf_sb[:], idf[:], s_w)
        p.wait("tensor", s_w, 48)
        p.wait("vector", s_w, 48)
        p.wait("tensor", s_c0, 1)

        for half in range(2):
            p.wait_now("gpsimd", s_pm)
            p.dma("gpsimd", wq_sb[:].rearrange("p a b -> p (a b)"),
                  wqp[half].rearrange("p a b -> p (a b)"), s_wp)
            p.dma("gpsimd", wk_sb[:].rearrange("p a b -> p (a b)"),
                  wkp[half].rearrange("p a b -> p (a b)"), s_wp)
            p.dma("gpsimd", wv_sb[:].rearrange("p a b -> p (a b)"),
                  wvp[half].rearrange("p a b -> p (a b)"), s_wp)
            base_w = p.cnt[s_wp]
            for lc in range(8):
                p.wait_now("gpsimd", s_pm)
                p.dma("gpsimd", xn_sb[:].rearrange("p a b -> p (a b)"),
                      xni[:, :, lc * 512:(lc + 1) * 512], s_x)
                p.wait("tensor", s_wp, base_w)
                p.wait("tensor", s_x, p.cnt[s_x])
                if half + lc > 0:
                    p.wait("tensor", s_pe, p.cnt[s_pe])
                for wsb, tag in ((wq_sb, 0), (wk_sb, 1)):
                    for dot in range(4):
                        bank = tag * 4 + dot
                        for d in range(16):
                            p.emit("tensor", lambda e, wsb=wsb, dot=dot, d=d, bank=bank: e.matmul(
                                ps[:, bank], wsb[:, d, dot * 128:(dot + 1) * 128],
                                xn_sb[:, d, 0:512],
                                start=(d == 0), stop=(d == 15)),
                                inc=(s_pm if d == 15 else None))
                p.wait_now("vector", s_pm)
                p.wait("vector", s_pst, p.cnt[s_pst])
                for tag in range(2):
                    for dot in range(4):
                        bank = tag * 4 + dot
                        p.emit("vector", lambda e, tag=tag, dot=dot, bank=bank: e.tensor_copy(
                            st1_sb[:, tag * 4 + dot, :], ps[:, bank]), inc=s_pe)
                p.wait_now("sync", s_pe)
                for dot in range(4):
                    hsl = half * 4 + dot // 1
                    p.dma("sync", qs[half * 4 + dot, :, lc * 512:(lc + 1) * 512],
                          st1_sb[:, dot, :], s_pst)
                    p.dma("sync", ks[half * 4 + dot, :, lc * 512:(lc + 1) * 512],
                          st1_sb[:, 4 + dot, :], s_pst)
                p.wait_now("tensor", s_pe)
                for lt in range(4):
                    bank = lt
                    for d in range(16):
                        p.emit("tensor", lambda e, lt=lt, d=d, bank=bank: e.matmul(
                            ps[:, bank], xn_sb[:, d, lt * 128:(lt + 1) * 128],
                            wv_sb[:, d, 0:512],
                            start=(d == 0), stop=(d == 15)),
                            inc=(s_pm if d == 15 else None))
                p.wait_now("vector", s_pm)
                p.wait("vector", s_pst, p.cnt[s_pst])
                for lt in range(4):
                    p.emit("vector", lambda e, lt=lt: e.tensor_copy(
                        st1_sb[:, 4 + lt, :], ps[:, lt]), inc=s_pe)
                p.wait_now("sync", s_pe)
                for lt in range(4):
                    p.dma("sync", vts[lc * 4 + lt, :, half * 512:(half + 1) * 512],
                          st1_sb[:, 4 + lt, :], s_pst)
        p.barrier()
        p.wait_now("gpsimd", s_pst)

        for h in range(8):
            p.wait_now("gpsimd", s_tmm)
            p.wait_now("gpsimd", s_cmm)
            p.wait_now("gpsimd", s_smm)
            p.dma("gpsimd", k_sb[:, :], ks[h], s_kl)
            p.dma("gpsimd", q_sb[:, :], qs[h], s_kl)
            p.dma("gpsimd", vt_sb[:],
                  vts[:, :, h * 128:(h + 1) * 128].rearrange("a p b -> p a b"), s_kl)
            p.wait("tensor", s_kl, p.cnt[s_kl])
            for g in range(8):
                nkt = 4 * (g + 1)
                for sub in range(4):
                    qt = 4 * g + sub
                    nch = (qt * 128) // 512
                    rem = qt * 128 - nch * 512
                    nmx = 0
                    for ch in range(nch + 1):
                        w = 512 if ch < nch else rem + 128
                        if w == 0:
                            continue
                        bank = ch % 2
                        p.wait("tensor", s_pe, p.cnt[s_pe])
                        p.emit("tensor", lambda e, qt=qt, ch=ch, w=w, bank=bank: e.matmul(
                            ps[:, bank, 0:w], q_sb[:, qt * 128:(qt + 1) * 128],
                            k_sb[:, ch * 512:ch * 512 + w],
                            start=True, stop=True), inc=s_smm)
                        p.wait("vector", s_smm, p.cnt[s_smm])
                        full_w = w - 128 if ch == nch else w
                        if ch == nch:
                            p.emit("vector", lambda e, bank=bank, off=full_w: e.tensor_add(
                                dtmp_sb[:], ps[:, bank, off:off + 128], mko_sb[:]))
                            p.emit("vector", lambda e, nmx=nmx: e.tensor_reduce(
                                mx_sb[:, nmx:nmx + 1], dtmp_sb[:],
                                mybir.AxisListType.X, ALU.max), inc=s_pe)
                            nmx += 1
                        if full_w > 0:
                            p.emit("vector", lambda e, bank=bank, fw=full_w, nmx=nmx: e.tensor_reduce(
                                mx_sb[:, nmx:nmx + 1], ps[:, bank, 0:fw],
                                mybir.AxisListType.X, ALU.max), inc=s_pe)
                            nmx += 1
                    p.emit("vector", lambda e, nmx=nmx: e.tensor_reduce(
                        mx_sb[:, 11:12], mx_sb[:, 0:nmx], mybir.AxisListType.X, ALU.max))
                    p.emit("vector", lambda e, sub=sub: e.tensor_scalar_mul(
                        neg_sb[:, sub:sub + 1], mx_sb[:, 11:12], -ISCALE), inc=s_pe)
                p.wait_now("tensor", s_pe)
                p.emit("tensor", lambda e: e.transpose(
                    ps[0:4, 6, 0:128], neg_sb[:, 0:4], idf_sb[:]), inc=s_tmm)
                p.wait_now("vector", s_tmm)
                p.emit("vector", lambda e: e.tensor_copy(dtmp_sb[0:4, 0:128], ps[0:4, 6, 0:128]),
                       inc=s_tev)
                p.wait_now("sync", s_tev)
                p.wait_now("sync", s_tmm)
                for sub in range(4):
                    p.dma("sync", srow_sb[0:1, 0, sub * 128:(sub + 1) * 128],
                          dtmp_sb[sub:sub + 1, 0:128], s_sst)
                p.wait("tensor", s_sst, p.cnt[s_sst])
                p.wait("tensor", s_tev, p.cnt[s_tev])
                p.emit("tensor", lambda e: e.matmul(
                    ps[:, 6, :], onesf_sb[0:1, :], srow_sb[0:1, 0, :],
                    start=True, stop=True), inc=s_tmm)
                p.wait_now("vector", s_tmm)
                p.emit("vector", lambda e: e.tensor_copy(rrep_sb[:], ps[:, 6, :]), inc=s_tev)
                for kt in range(nkt):
                    bank = 2 + kt % 2
                    p.wait("tensor", s_tev, p.cnt[s_tev])
                    p.emit("tensor", lambda e, kt=kt, g=g, bank=bank: e.matmul(
                        ps[:, bank], k_sb[:, kt * 128:(kt + 1) * 128],
                        q_sb[:, g * 512:(g + 1) * 512],
                        start=True, stop=True), inc=s_tmm)
                    p.wait("vector", s_tmm, p.cnt[s_tmm])
                    p.wait("vector", s_pT, p.cnt[s_pT])
                    p.emit("vector", lambda e, bank=bank: e.scalar_tensor_tensor(
                        stt_sb[:], ps[:, bank], ISCALE, rrep_sb[:], ALU.mult, ALU.add),
                        inc=s_tev)
                    qcut = (kt - 4 * g) * 128
                    if qcut > 0:
                        p.emit("vector", lambda e, qcut=qcut: e.tensor_scalar_add(
                            stt_sb[:, 0:qcut], stt_sb[:, 0:qcut], -1e9))
                    if 0 <= kt - 4 * g < 4:
                        p.emit("vector", lambda e, qcut=qcut: e.tensor_add(
                            stt_sb[:, qcut:qcut + 128], stt_sb[:, qcut:qcut + 128],
                            mkt_sb[:]))
                    p.emit("vector", lambda e: e.engine_nop(), inc=s_tev)
                    p.wait("scalar", s_tev, p.cnt[s_tev])
                    p.wait("scalar", s_cmm, p.cnt[s_cmm])
                    p.emit("scalar", lambda e, kt=kt: e.activation(
                        pT_sb[:, kt, :], stt_sb[:], AF.Exp), inc=s_pT)
                p.wait_now("tensor", s_pT)
                p.wait_now("tensor", s_pe)
                p.wait_now("tensor", s_nrm)
                for kt in range(nkt):
                    p.emit("tensor", lambda e, kt=kt, nkt=nkt: e.matmul(
                        ps[0:1, 7, 0:512], onesb_sb[:, 0:1], pT_sb[:, kt, :],
                        start=(kt == 0), stop=(kt == nkt - 1)),
                        inc=(s_sum if kt == nkt - 1 else None))
                for kt in range(nkt):
                    p.emit("tensor", lambda e, kt=kt, nkt=nkt: e.matmul(
                        ps[:, 4, :], vt_sb[:, kt, :], pT_sb[:, kt, :],
                        start=(kt == 0), stop=(kt == nkt - 1)),
                        inc=(s_cmm if kt == nkt - 1 else None))
                p.wait_now("vector", s_sum)
                p.emit("vector", lambda e: e.reciprocal(srow_sb[0:1, 1, :], ps[0:1, 7, 0:512]),
                       inc=s_nrm)
                p.wait_now("tensor", s_nrm)
                p.emit("tensor", lambda e: e.matmul(
                    ps[:, 5, :], onesf_sb[0:1, :], srow_sb[0:1, 1, :],
                    start=True, stop=True), inc=s_tmm)
                p.wait_now("vector", s_tmm)
                p.emit("vector", lambda e: e.tensor_copy(rrep_sb[:], ps[:, 5, :]))
                p.wait_now("vector", s_cmm)
                p.emit("vector", lambda e, h=h, g=g: e.tensor_tensor(
                    ctx_sb[:, h, g, :], ps[:, 4, :], rrep_sb[:], ALU.mult), inc=s_pe)
        p.barrier()

        for sl in range(4):
            p.wait_now("gpsimd", s_omm)
            p.dma("gpsimd", wo_sb[:, sl].rearrange("p a b -> p (a b)") if False else
                  wo_sb[:, sl, :, :].rearrange("p a b -> p (a b)"),
                  wop[sl].rearrange("p a b -> p (a b)"), s_wo)
        p.wait("tensor", s_wo, 64)
        kk3 = 0
        for g in range(8):
            for sl in range(4):
                for dot in range(4):
                    bank = kk3 % 8
                    dglob = sl * 4 + dot
                    if kk3 >= 8:
                        p.wait("tensor", s_oev, kk3 - 7)
                    kk3 += 1
                    for hch in range(8):
                        p.emit("tensor", lambda e, sl=sl, dot=dot, hch=hch, g=g, bank=bank: e.matmul(
                            ps[:, bank], wo_sb[:, sl, hch, dot * 128:(dot + 1) * 128],
                            ctx_sb[:, hch, g, :],
                            start=(hch == 0), stop=(hch == 7)),
                            inc=(s_omm if hch == 7 else None))
                    p.wait("vector", s_omm, p.cnt[s_omm])
                    p.emit("vector", lambda e, dglob=dglob, bank=bank: e.tensor_copy(
                        po_sb[:, dglob, :], ps[:, bank]), inc=s_oev)
            p.wait_now("sync", s_oev)
            if g >= 1:
                p.wait("sync", s_ost, 16 * g)
            p.dma("sync", po[:, :, g * 512:(g + 1) * 512], po_sb[:], s_ost)
            p.wait_now("vector", s_ost)

        for e in ENGINES:
            p.wait_now(e, s_ost)
        with nc.Block() as blk:
            p.build(blk)
    return nc



def build_l3():
    from contextlib import ExitStack
    nc = bass.Bass("TRN2", target_bir_lowering=False, debug=False)
    NLL = L
    s2f = nc.dram_tensor("s2f", [128, 16, NLL], F32, kind="ExternalInput").ap()
    wqp = nc.dram_tensor("wqp", [1, 128, 16, 512], F32, kind="ExternalInput").ap()
    wkp = nc.dram_tensor("wkp", [2, 128, 16, 512], F32, kind="ExternalInput").ap()
    wvp = nc.dram_tensor("wvp", [2, 128, 16, 512], F32, kind="ExternalInput").ap()
    wgp = nc.dram_tensor("wgp", [1, 128, 16, 512], F32, kind="ExternalInput").ap()
    wbp = nc.dram_tensor("wbp", [128, 16, 8], F32, kind="ExternalInput").ap()
    wap = nc.dram_tensor("wap", [128, 16, 4], F32, kind="ExternalInput").ap()
    wop = nc.dram_tensor("wop", [4, 128, 4, 512], F32, kind="ExternalInput").ap()
    cvq = nc.dram_tensor("cvq", [128, 4, 4], F32, kind="ExternalInput").ap()
    cvk = nc.dram_tensor("cvk", [128, 8, 4], F32, kind="ExternalInput").ap()
    cvv = nc.dram_tensor("cvv", [128, 8, 4], F32, kind="ExternalInput").ap()
    alg = nc.dram_tensor("alg", [4, 1], F32, kind="ExternalInput").ap()
    dtb = nc.dram_tensor("dtb", [4, 1], F32, kind="ExternalInput").ap()
    onw = nc.dram_tensor("onw", [1, 128], F32, kind="ExternalInput").ap()
    mskSN = nc.dram_tensor("mskSN", [128, 128], F32, kind="ExternalInput").ap()
    mskSTN = nc.dram_tensor("mskSTN", [128, 128], F32, kind="ExternalInput").ap()
    mskIT = nc.dram_tensor("mskIT", [128, 128], F32, kind="ExternalInput").ap()
    idf = nc.dram_tensor("idf", [128, 128], F32, kind="ExternalInput").ap()
    idb = nc.dram_tensor("idb", [128, 128], F32, kind="ExternalInput").ap()
    pl3 = nc.dram_tensor("pl3", [128, 16, NLL], BF16, kind="ExternalOutput").ap()
    qhS = nc.dram_tensor("qhS", [4, 128, NLL], BF16, kind="Internal").ap()
    kfS = nc.dram_tensor("kfS", [8, 128, NLL], BF16, kind="Internal").ap()
    vfS = nc.dram_tensor("vfS", [8, 128, NLL], BF16, kind="Internal").ap()
    rawS = nc.dram_tensor("rawS", [20, 128, NLL], BF16, kind="Internal").ap()
    ggS = nc.dram_tensor("ggS", [32, 128, 512], BF16, kind="Internal").ap()
    bS = nc.dram_tensor("bS", [8, NLL], F32, kind="Internal").ap()
    gS = nc.dram_tensor("gS", [4, NLL], F32, kind="Internal").ap()
    oS = nc.dram_tensor("oS", [4, NLL, 128], BF16, kind="Internal").ap()

    NCH = NLL // 64
    with ExitStack() as st:
        p = Prog(nc, st)
        sb = lambda name, shape, dt: st.enter_context(nc.sbuf_tensor(name, shape, dt))
        onesf_sb = sb("onesf", [128, 128], F32)
        cons_sb = sb("cons", [128, 4], F32)
        idf_sb = sb("idfS", [128, 128], F32)
        idb_sb = sb("idbS", [128, 128], BF16)
        mskSN_sb = sb("mS", [128, 128], F32)
        mskSTN_sb = sb("mST", [128, 128], F32)
        mskIT_sb = sb("mIT", [128, 128], F32)
        alg_sb = sb("algS", [4, 2], F32)
        dtb_sb = sb("dtbS", [4, 1], F32)
        onw_sb = sb("onwS", [1, 128], F32)
        cq_sb = sb("cq", [128, 4, 4], F32)
        ck_sb = sb("ck", [128, 8, 4], F32)
        cv_sb = sb("cv", [128, 8, 4], F32)
        wb_sb = sb("wbS", [128, 16, 8], BF16)
        wa_sb = sb("waS", [128, 16, 4], BF16)
        w_sb = sb("wsl", [128, 16, 512], BF16)
        x_sb = sb("xsl", [128, 16, 512], BF16)
        stg_sb = sb("stg", [128, 8, 512], BF16)
        pad_sb = sb("pad", [128, 4100], BF16)
        cvt_sb = sb("cvt", [128, 4096], F32)
        cv2_sb = sb("cv2", [128, 4096], BF16)
        sqc_sb = sb("sqc", [128, 512], F32)
        rowq_sb = sb("rowq", [4, 4, 512], F32)
        gr_sb = sb("gr", [1, 4, 128], F32)
        br_sb = sb("br", [1, 2, 128], F32)
        stk_sb = sb("stk", [4, 128], F32)
        cols_sb = sb("cols", [128, 6], F32)
        kc_sb = sb("kc", [128, 128], BF16)
        vc_sb = sb("vc", [128, 128], BF16)
        qc_sb = sb("qc", [128, 128], BF16)
        qcf_sb = sb("qcf", [128, 128], F32)
        kt_sb = sb("kt", [128, 128], F32)
        vt_sb = sb("vt", [128, 128], F32)
        dd_sb = sb("dd", [128, 128], F32)
        dd2_sb = sb("dd2", [128, 128], F32)
        dmS_sb = sb("dmS", [128, 128], F32)
        dmST_sb = sb("dmST", [128, 128], F32)
        dqkt_sb = sb("dqkt", [128, 128], F32)
        brep_sb = sb("brp", [128, 128], F32)
        aa_sb = sb("aa", [128, 2, 128], F32)
        aat_sb = sb("aat", [128, 2, 128], F32)
        yy_sb = sb("yy", [128, 256], F32)
        khat_sb = sb("khat", [128, 128], F32)
        nat_sb = sb("nat", [128, 128], F32)
        ptp_sb = sb("ptp", [128, 128], F32)
        ww_sb = sb("ww", [128, 128], F32)
        ss_sb = sb("ss", [128, 2, 128], F32)
        t2_sb = sb("t2", [128, 128], F32)
        och_sb = sb("och", [128, 128], F32)
        ob_sb = sb("ob", [64, 128], BF16)
        ot_sb = sb("ot", [128, 4, 128], BF16)
        wo_sb = sb("woS", [128, 4, 4, 512], BF16)
        po_sb = sb("poS", [128, 16, 128], BF16)
        gg_sb = sb("ggt", [128, 512], BF16)
        ps = st.enter_context(nc.psum_tensor("ps", [128, 6, 512], F32))
        psb = st.enter_context(nc.psum_tensor("psb", [128, 2, 1024], BF16))

        s_w = p.sem("s_w"); s_c0 = p.sem("s_c0"); s_x = p.sem("s_x")
        s_ws = p.sem("s_ws"); s_pm = p.sem("s_pm"); s_pe = p.sem("s_pe")
        s_pst = p.sem("s_pst")
        s_rl = p.sem("s_rl"); s_cm = p.sem("s_cm"); s_ce = p.sem("s_ce")
        s_cst = p.sem("s_cst2")
        s_kl = p.sem("s_kl"); s_tm = p.sem("s_tm"); s_tv = p.sem("s_tv")
        s_sc = p.sem("s_sc"); s_ost = p.sem("s_ost")
        s_ol = p.sem("s_ol"); s_om = p.sem("s_om")
        s_oe = p.sem("s_oe"); s_fst = p.sem("s_fst"); s_wo = p.sem("s_wo")
        s_stk = p.sem("s_stk"); s_pe2 = p.sem("s_pe2"); s_pe3 = p.sem("s_pe3"); s_rl2 = p.sem("s_rl2"); s_rl3 = p.sem("s_rl3")
        s_cmS = p.sem("s_cmS"); s_cmT = p.sem("s_cmT"); s_cmV = p.sem("s_cmV")
        s_scV = p.sem("s_scV"); s_scS = p.sem("s_scS")

        p.emit("vector", lambda e: e.memset(onesf_sb[:], 1.0))
        p.emit("vector", lambda e: e.memset(cons_sb[:, 0:1], -1.0))
        p.emit("vector", lambda e: e.memset(cons_sb[:, 1:2], 1.0))
        p.emit("vector", lambda e: e.memset(cons_sb[:, 2:3], -0.5))
        p.emit("vector", lambda e: e.memset(cons_sb[:, 3:4], float(EPS)), inc=s_c0)
        for dst, srcn in ((idf_sb[:], idf), (idb_sb[:], idb), (mskSN_sb[:], mskSN),
                          (mskSTN_sb[:], mskSTN), (mskIT_sb[:], mskIT),
                          (alg_sb[0:4, 0:1], alg), (dtb_sb[:], dtb), (onw_sb[:], onw),
                          (cq_sb[:], cvq), (ck_sb[:], cvk), (cv_sb[:], cvv),
                          (wb_sb[:], wbp), (wa_sb[:], wap)):
            p.dma("gpsimd", dst, srcn, s_w)
        NW = 16 * 13
        for e in ("tensor", "scalar", "vector"):
            p.wait(e, s_w, NW)
            p.wait(e, s_c0, 1)
        p.emit("scalar", lambda e: e.activation(alg_sb[0:4, 1:2], alg_sb[0:4, 0:1], AF.Exp),
               inc=s_c0)
        p.wait("vector", s_c0, 2)
        p.emit("vector", lambda e: e.tensor_scalar_mul(alg_sb[0:4, 1:2], alg_sb[0:4, 1:2], -1.0),
               inc=s_c0)
        p.wait("scalar", s_c0, 3)

        for lc in range(8):
            p.wait_now("gpsimd", s_pm)
            p.dma("gpsimd", x_sb[:].rearrange("p a b -> p (a b)"),
                  s2f[:, :, lc * 512:(lc + 1) * 512], s_x)
            for pname, wsrc, nsl, base in (("q", wqp, 1, 0), ("k", wkp, 2, 4), ("v", wvp, 2, 12)):
                for sl in range(nsl):
                    p.wait_now("gpsimd", s_pe)
                    p.dma("gpsimd", w_sb[:].rearrange("p a b -> p (a b)"),
                          wsrc[sl].rearrange("p a b -> p (a b)"), s_ws)
                    p.wait("tensor", s_ws, p.cnt[s_ws])
                    p.wait("tensor", s_x, p.cnt[s_x])
                    p.wait_now("tensor", s_pe)
                    for dot in range(4):
                        for d in range(16):
                            p.emit("tensor", lambda e, dot=dot, d=d: e.matmul(
                                ps[:, dot], w_sb[:, d, dot * 128:(dot + 1) * 128],
                                x_sb[:, d, :], start=(d == 0), stop=(d == 15)),
                                inc=(s_pm if d == 15 else None))
                    p.wait_now("vector", s_pm)
                    p.wait("vector", s_pst, p.cnt[s_pst])
                    for dot in range(4):
                        p.emit("vector", lambda e, dot=dot: e.tensor_copy(
                            stg_sb[:, dot, :], ps[:, dot]), inc=s_pe)
                    p.wait_now("sync", s_pe)
                    for dot in range(4):
                        p.dma("sync", rawS[base + sl * 4 + dot, :, lc * 512:(lc + 1) * 512],
                              stg_sb[:, dot, :], s_pst)
            p.wait_now("gpsimd", s_pe)
            p.dma("gpsimd", w_sb[:].rearrange("p a b -> p (a b)"),
                  wgp[0].rearrange("p a b -> p (a b)"), s_ws)
            p.wait("tensor", s_ws, p.cnt[s_ws])
            p.wait_now("tensor", s_pe)
            for lt in range(4):
                for d in range(16):
                    p.emit("tensor", lambda e, lt=lt, d=d: e.matmul(
                        ps[:, lt], x_sb[:, d, lt * 128:(lt + 1) * 128],
                        w_sb[:, d, :], start=(d == 0), stop=(d == 15)),
                        inc=(s_pm if d == 15 else None))
            for d in range(16):
                p.emit("tensor", lambda e, d=d: e.matmul(
                    ps[0:8, 4, :], wb_sb[:, d, :], x_sb[:, d, :],
                    start=(d == 0), stop=(d == 15)), inc=(s_pm if d == 15 else None))
            for d in range(16):
                p.emit("tensor", lambda e, d=d: e.matmul(
                    ps[0:4, 5, :], wa_sb[:, d, :], x_sb[:, d, :],
                    start=(d == 0), stop=(d == 15)), inc=(s_pm if d == 15 else None))
            p.wait_now("vector", s_pm)
            p.wait("vector", s_pst, p.cnt[s_pst])
            for lt in range(4):
                p.emit("vector", lambda e, lt=lt: e.tensor_copy(
                    stg_sb[:, lt, :], ps[:, lt]), inc=s_pe)
            p.wait_now("scalar", s_pm)
            p.emit("scalar", lambda e: e.activation(sqc_sb[0:8, 0:512], ps[0:8, 4, :], AF.Sigmoid),
                   inc=s_pe2)
            p.wait_now("vector", s_pe2)
            p.emit("vector", lambda e: e.tensor_scalar_mul(
                sqc_sb[0:8, 0:512], sqc_sb[0:8, 0:512], 2.0), inc=s_pe3)
            p.emit("scalar", lambda e: e.activation(rowq_sb[0:4, 0, :], ps[0:4, 5, :],
                                                    AF.Abs, bias=dtb_sb[0:4, 0:1]))
            p.emit("scalar", lambda e: e.activation(rowq_sb[0:4, 1, :], rowq_sb[0:4, 0, :],
                                                    AF.Exp, scale=cons_sb[0:4, 0:1]))
            p.emit("scalar", lambda e: e.activation(rowq_sb[0:4, 1, :], rowq_sb[0:4, 1, :],
                                                    AF.Ln, bias=cons_sb[0:4, 1:2]), inc=s_pe2)
            p.wait_now("vector", s_pe2)
            p.emit("vector", lambda e: e.tensor_scalar(rowq_sb[0:4, 2, :], ps[0:4, 5, :],
                                                       dtb_sb[0:4, 0:1], 0.0, ALU.add, ALU.max))
            p.emit("vector", lambda e: e.tensor_add(rowq_sb[0:4, 2, :], rowq_sb[0:4, 2, :],
                                                    rowq_sb[0:4, 1, :]))
            p.emit("vector", lambda e: e.tensor_scalar(rowq_sb[0:4, 2, :], rowq_sb[0:4, 2, :],
                                                       alg_sb[0:4, 1:2], None, ALU.mult, ALU.bypass),
                   inc=s_pe3)
            p.wait_now("sync", s_pe)
            p.wait_now("sync", s_pe3)
            for lt in range(4):
                p.dma("sync", ggS[lc * 4 + lt], stg_sb[:, lt, :], s_pst)
            p.dma("sync", bS[:, lc * 512:(lc + 1) * 512], sqc_sb[0:8, 0:512], s_pst)
            p.dma("sync", gS[:, lc * 512:(lc + 1) * 512], rowq_sb[0:4, 2, :], s_pst)
        p.barrier()
        p.wait_now("gpsimd", s_pst)

        for ti in range(20):
            is_q = ti < 4
            is_k = 4 <= ti < 12
            cw = cq_sb[:, ti, :] if is_q else (ck_sb[:, ti - 4, :] if is_k else cv_sb[:, ti - 12, :])
            p.wait_now("vector", s_cst)
            p.emit("vector", lambda e: e.memset(pad_sb[:, 0:4], 0.0), inc=s_rl2)
            p.wait_now("gpsimd", s_rl2)
            p.wait_now("gpsimd", s_rl3)
            p.dma("gpsimd", pad_sb[:, 4:4100], rawS[ti], s_rl)
            p.wait("vector", s_rl, p.cnt[s_rl])
            p.emit("vector", lambda e, cw=cw: e.tensor_scalar(
                cvt_sb[:], pad_sb[:, 1:4097], cw[:, 0:1], None, ALU.mult, ALU.bypass))
            for tap in range(1, 4):
                p.emit("vector", lambda e, cw=cw, tap=tap: e.scalar_tensor_tensor(
                    cvt_sb[:], pad_sb[:, 1 + tap:4097 + tap], cw[:, tap:tap + 1],
                    cvt_sb[:], ALU.mult, ALU.add))
            p.emit("vector", lambda e: e.engine_nop(), inc=s_rl2)
            p.wait_now("scalar", s_rl2)
            p.wait_now("scalar", s_rl)
            p.emit("scalar", lambda e: e.activation(cv2_sb[:], cvt_sb[:], AF.Silu), inc=s_rl3)
            p.wait_now("vector", s_rl3)
            if is_q or is_k:
                for ch in range(8):
                    p.emit("scalar", lambda e, ch=ch: e.activation(
                        sqc_sb[:], cv2_sb[:, ch * 512:(ch + 1) * 512], AF.Square), inc=s_cmS)
                    p.wait_now("tensor", s_cmS)
                    p.wait_now("tensor", s_ce)
                    p.emit("tensor", lambda e: e.matmul(
                        ps[0:1, 4, :], onesf_sb[:, 0:1], sqc_sb[:], start=True, stop=True),
                        inc=s_cmT)
                    p.wait_now("scalar", s_cmT)
                    p.emit("scalar", lambda e: e.activation(rowq_sb[0:1, 0, :], ps[0:1, 4, :],
                                                            AF.Sqrt), inc=s_cmS)
                    p.wait_now("vector", s_cmS)
                    p.emit("vector", lambda e: e.tensor_scalar_max(
                        rowq_sb[0:1, 1, :], rowq_sb[0:1, 0, :], 1e-12))
                    p.emit("vector", lambda e: e.reciprocal(rowq_sb[0:1, 2, :],
                                                            rowq_sb[0:1, 1, :]))
                    if is_q:
                        p.emit("vector", lambda e: e.tensor_scalar_mul(
                            rowq_sb[0:1, 2, :], rowq_sb[0:1, 2, :],
                            float(1.0 / np.sqrt(128.0))))
                    p.emit("vector", lambda e: e.engine_nop(), inc=s_cmV)
                    p.wait_now("tensor", s_cmV)
                    p.emit("tensor", lambda e: e.matmul(
                        ps[:, 5, :], onesf_sb[0:1, :], rowq_sb[0:1, 2, :],
                        start=True, stop=True), inc=s_cmT)
                    p.wait_now("vector", s_cmT)
                    p.emit("vector", lambda e, ch=ch: e.tensor_mul(
                        cv2_sb[:, ch * 512:(ch + 1) * 512], cv2_sb[:, ch * 512:(ch + 1) * 512],
                        ps[:, 5, :]), inc=s_ce)
            p.emit("vector", lambda e: e.engine_nop(), inc=s_scV)
            p.wait_now("sync", s_scV)
            dst = qhS[ti] if is_q else (kfS[ti - 4] if is_k else vfS[ti - 12])
            p.dma("sync", dst, cv2_sb[:], s_cst)
        p.barrier()
        p.wait_now("gpsimd", s_cst)

        for hl in range(4):
            p.wait_now("vector", s_ost)
            p.emit("vector", lambda e: e.memset(ss_sb[:].rearrange("p a b -> p (a b)"), 0.0))
            p.emit("vector", lambda e: e.memset(qc_sb[:, 0:64], 0.0), inc=s_scV)
            for cc in range(NCH):
                t0 = cc * 64
                cur, oth = cc % 2, (cc + 1) % 2
                p.wait_now("gpsimd", s_tm)
                p.wait_now("gpsimd", s_tv)
                p.dma("gpsimd", kc_sb[:, 0:64], kfS[hl, :, t0:t0 + 64], s_kl)
                p.dma("gpsimd", kc_sb[:, 64:128], kfS[4 + hl, :, t0:t0 + 64], s_kl)
                p.dma("gpsimd", vc_sb[:, 0:64], vfS[hl, :, t0:t0 + 64], s_kl)
                p.dma("gpsimd", vc_sb[:, 64:128], vfS[4 + hl, :, t0:t0 + 64], s_kl)
                p.dma("gpsimd", qc_sb[:, 64:128], qhS[hl, :, t0:t0 + 64], s_kl)
                p.dma("gpsimd", br_sb[0:1, 0, 0:64], bS[hl:hl + 1, t0:t0 + 64], s_kl)
                p.dma("gpsimd", br_sb[0:1, 0, 64:128], bS[4 + hl:5 + hl, t0:t0 + 64], s_kl)
                p.dma("gpsimd", br_sb[0:1, 1, 0:64], gS[hl:hl + 1, t0:t0 + 64], s_kl)
                p.wait("vector", s_kl, p.cnt[s_kl])
                p.emit("vector", lambda e: e.tensor_tensor_scan(
                    gr_sb[0:1, 0, 0:64], onesf_sb[0:1, 0:64], br_sb[0:1, 1, 0:64], 0.0,
                    ALU.mult, ALU.add))
                p.emit("vector", lambda e: e.tensor_copy(gr_sb[0:1, 0, 64:128],
                                                         gr_sb[0:1, 0, 0:64]), inc=s_scV)
                p.wait_now("scalar", s_scV)
                p.emit("scalar", lambda e: e.activation(gr_sb[0:1, 1, :], gr_sb[0:1, 0, :], AF.Exp))
                p.emit("scalar", lambda e: e.activation(
                    gr_sb[0:1, 2, :], gr_sb[0:1, 0, :], AF.Exp, scale=cons_sb[0:1, 0:1],
                    bias=gr_sb[0:1, 0, 127:128]), inc=s_scS)
                p.wait_now("vector", s_scS)
                p.emit("vector", lambda e: e.tensor_mul(gr_sb[0:1, 3, :], br_sb[0:1, 0, :],
                                                        gr_sb[0:1, 1, :]), inc=s_scV)
                p.wait_now("sync", s_scV)
                p.wait_now("sync", s_scS)
                p.wait_now("sync", s_stk)
                p.dma("sync", stk_sb[0:1, :], br_sb[0:1, 0, :], s_cst)
                p.dma("sync", stk_sb[1:2, :], gr_sb[0:1, 3, :], s_cst)
                p.dma("sync", stk_sb[2:3, :], gr_sb[0:1, 2, :], s_cst)
                p.dma("sync", stk_sb[3:4, :], gr_sb[0:1, 0, :], s_cst)
                p.wait("tensor", s_cst, p.cnt[s_cst])
                p.wait_now("tensor", s_tv)
                p.emit("tensor", lambda e: e.transpose(ps[:, 5, 0:4], stk_sb[:], idf_sb[0:4, 0:4]),
                       inc=s_tm)
                p.wait("vector", s_tm, p.cnt[s_tm])
                p.emit("vector", lambda e: e.tensor_copy(cols_sb[:, 0:4], ps[:, 5, 0:4]),
                       inc=s_stk)
                p.wait("scalar", s_stk, p.cnt[s_stk])
                p.emit("scalar", lambda e: e.activation(cols_sb[:, 5:6], cols_sb[:, 3:4], AF.Exp),
                       inc=s_scS)
                p.wait("tensor", s_kl, p.cnt[s_kl])
                p.emit("tensor", lambda e: e.transpose(psb[:, 0, 0:128], kc_sb[:], idb_sb[:]),
                       inc=s_tm)
                p.emit("tensor", lambda e: e.transpose(psb[:, 1, 0:128], vc_sb[:], idb_sb[:]),
                       inc=s_tm)
                p.wait("vector", s_tm, p.cnt[s_tm])
                p.emit("vector", lambda e: e.tensor_copy(kt_sb[:], psb[:, 0, 0:128]))
                p.emit("vector", lambda e: e.tensor_copy(vt_sb[:], psb[:, 1, 0:128]), inc=s_tv)
                p.wait_now("tensor", s_oe)
                p.emit("tensor", lambda e: e.matmul(ps[:, 0, 0:128], kc_sb[:], kc_sb[:],
                                                    start=True, stop=True))
                p.emit("tensor", lambda e: e.matmul(ps[:, 1, 0:128], kc_sb[:], qc_sb[:],
                                                    start=True, stop=True))
                p.emit("tensor", lambda e: e.matmul(ps[:, 2, 0:128], onesf_sb[0:1, :],
                                                    gr_sb[0:1, 0, :], start=True, stop=True))
                p.emit("tensor", lambda e: e.matmul(ps[:, 3, 0:128], onesf_sb[0:1, :],
                                                    br_sb[0:1, 0, :], start=True, stop=True),
                       inc=s_tm)
                p.wait("vector", s_tm, p.cnt[s_tm])
                p.emit("vector", lambda e: e.tensor_scalar(
                    dd_sb[:], ps[:, 2, 0:128], -1.0, cols_sb[:, 3:4], ALU.mult, ALU.add))
                p.emit("vector", lambda e: e.tensor_scalar_mul(dd2_sb[:], dd_sb[:], -1.0))
                p.emit("vector", lambda e: e.tensor_scalar_min(dd_sb[:], dd_sb[:], 0.0))
                p.emit("vector", lambda e: e.tensor_scalar_min(dd2_sb[:], dd2_sb[:], 0.0),
                       inc=s_scV)
                p.wait_now("scalar", s_scV)
                p.emit("scalar", lambda e: e.activation(dd_sb[:], dd_sb[:], AF.Exp))
                p.emit("scalar", lambda e: e.activation(dd2_sb[:], dd2_sb[:], AF.Exp), inc=s_scS)
                p.wait_now("vector", s_scS)
                p.emit("vector", lambda e: e.tensor_mul(dmS_sb[:], dd_sb[:], mskSN_sb[:]))
                p.emit("vector", lambda e: e.tensor_mul(dmST_sb[:], dd2_sb[:], mskSTN_sb[:]))
                p.emit("vector", lambda e: e.tensor_mul(dqkt_sb[:], dd2_sb[:], mskIT_sb[:]))
                p.emit("vector", lambda e: e.tensor_copy(brep_sb[:], ps[:, 3, 0:128]))
                p.emit("vector", lambda e: e.scalar_tensor_tensor(
                    aa_sb[:, 0, :], ps[:, 0, 0:128], cols_sb[:, 0:1], dmS_sb[:],
                    ALU.mult, ALU.mult))
                p.emit("vector", lambda e: e.tensor_mul(t2_sb[:], ps[:, 0, 0:128], dmST_sb[:]))
                p.emit("vector", lambda e: e.tensor_mul(aat_sb[:, 0, :], t2_sb[:], brep_sb[:]))
                p.emit("vector", lambda e: e.tensor_mul(dqkt_sb[:], ps[:, 1, 0:128], dqkt_sb[:]))
                p.emit("vector", lambda e: e.tensor_scalar(
                    yy_sb[:, 0:128], vt_sb[:], cols_sb[:, 0:1], None, ALU.mult, ALU.bypass))
                p.emit("vector", lambda e: e.tensor_scalar(
                    yy_sb[:, 128:256], kt_sb[:], cols_sb[:, 1:2], None, ALU.mult, ALU.bypass),
                    inc=s_tv)
                for k in range(7):
                    p.wait_now("tensor", s_tv)
                    p.emit("tensor", lambda e, k=k: e.matmul(
                        ps[:, 0, 0:256], aat_sb[:, k % 2, :], yy_sb[:],
                        start=True, stop=True), inc=s_tm)
                    if k < 6:
                        p.emit("tensor", lambda e, k=k: e.matmul(
                            ps[:, 1, 0:128], aat_sb[:, k % 2, :], aa_sb[:, k % 2, :],
                            start=True, stop=True))
                        p.emit("tensor", lambda e, k=k: e.matmul(
                            ps[:, 2, 0:128], aa_sb[:, k % 2, :], aat_sb[:, k % 2, :],
                            start=True, stop=True), inc=s_tm)
                    p.wait("vector", s_tm, p.cnt[s_tm])
                    p.emit("vector", lambda e: e.tensor_add(yy_sb[:], yy_sb[:], ps[:, 0, 0:256]))
                    if k < 6:
                        p.emit("vector", lambda e, k=k: e.tensor_copy(
                            aa_sb[:, (k + 1) % 2, :], ps[:, 1, 0:128]))
                        p.emit("vector", lambda e, k=k: e.tensor_copy(
                            aat_sb[:, (k + 1) % 2, :], ps[:, 2, 0:128]))
                    p.emit("vector", lambda e: e.engine_nop(), inc=s_tv)
                p.emit("vector", lambda e: e.tensor_scalar(
                    khat_sb[:], kt_sb[:], cols_sb[:, 2:3], None, ALU.mult, ALU.bypass), inc=s_tv)
                p.wait_now("tensor", s_tv)
                p.emit("tensor", lambda e: e.transpose(ps[:, 3, 0:128], yy_sb[:, 128:256],
                                                       idf_sb[:]))
                p.emit("tensor", lambda e: e.matmul(ps[:, 4, 0:128], yy_sb[:, 128:256],
                                                    khat_sb[:], start=True, stop=True), inc=s_tm)
                p.wait("vector", s_tm, p.cnt[s_tm])
                p.emit("vector", lambda e: e.tensor_copy(ptp_sb[:], ps[:, 3, 0:128]))
                p.emit("vector", lambda e: e.tensor_scalar_mul(nat_sb[:], ps[:, 4, 0:128], -1.0),
                       inc=s_tv)
                p.wait_now("tensor", s_tv)
                p.emit("tensor", lambda e: e.matmul(ps[:, 5, 0:1], onesf_sb[0:1, :],
                                                    gr_sb[0:1, 1, 127:128], start=True, stop=True),
                       inc=s_tm)
                p.wait("vector", s_tm, p.cnt[s_tm])
                p.emit("vector", lambda e: e.tensor_copy(cols_sb[:, 4:5], ps[:, 5, 0:1]), inc=s_tv)
                p.wait_now("tensor", s_tv)
                p.emit("tensor", lambda e, cur=cur: e.matmul(
                    ps[:, 0, 0:128], nat_sb[:], ss_sb[:, cur, :], start=True, stop=False))
                p.emit("tensor", lambda e: e.matmul(
                    ps[:, 0, 0:128], khat_sb[:], yy_sb[:, 0:128], start=False, stop=True),
                    inc=s_tm)
                p.emit("tensor", lambda e, cur=cur: e.matmul(
                    ps[:, 1, 0:128], ptp_sb[:], ss_sb[:, cur, :], start=True, stop=True),
                    inc=s_tm)
                p.emit("tensor", lambda e, cur=cur: e.matmul(
                    ps[:, 2, 0:128], qcf_sb[:], ss_sb[:, cur, :], start=True, stop=True),
                    inc=s_tm) if False else None
                p.wait("vector", s_tm, p.cnt[s_tm])
                p.emit("vector", lambda e, cur=cur, oth=oth: e.scalar_tensor_tensor(
                    ss_sb[:, oth, :], ss_sb[:, cur, :], cols_sb[:, 4:5], ps[:, 0, 0:128],
                    ALU.mult, ALU.add))
                p.emit("vector", lambda e: e.tensor_sub(ww_sb[:], yy_sb[:, 0:128],
                                                        ps[:, 1, 0:128]))
                p.emit("vector", lambda e: e.tensor_copy(qcf_sb[:], qc_sb[:]), inc=s_tv)
                p.wait_now("tensor", s_tv)
                p.emit("tensor", lambda e: e.matmul(ps[:, 3, 0:128], dqkt_sb[:], ww_sb[:],
                                                    start=True, stop=True))
                p.emit("tensor", lambda e, cur=cur: e.matmul(
                    ps[:, 4, 0:128], qcf_sb[:], ss_sb[:, cur, :], start=True, stop=True),
                    inc=s_tm)
                p.wait("vector", s_tm, p.cnt[s_tm])
                p.emit("vector", lambda e: e.tensor_scalar(
                    t2_sb[:], ps[:, 4, 0:128], cols_sb[:, 5:6], None, ALU.mult, ALU.bypass))
                p.emit("vector", lambda e: e.tensor_add(och_sb[:], ps[:, 3, 0:128], t2_sb[:]))
                p.wait("vector", s_ost, p.cnt[s_ost])
                p.emit("vector", lambda e: e.tensor_copy(ob_sb[:], och_sb[64:128, :]),
                       inc=s_oe)
                p.wait_now("sync", s_oe)
                p.dma("sync", oS[hl, t0:t0 + 64, :], ob_sb[:], s_ost)
        p.barrier()
        p.wait_now("gpsimd", s_ost)

        for sl in range(4):
            p.dma("gpsimd", wo_sb[:, sl].rearrange("p a b -> p (a b)"),
                  wop[sl].rearrange("p a b -> p (a b)"), s_wo)
        p.wait("tensor", s_wo, 64)
        p.wait_now("tensor", s_ce)
        p.emit("tensor", lambda e: e.matmul(ps[:, 5, 0:128], onesf_sb[0:1, :], onw_sb[0:1, :],
                                            start=True, stop=True), inc=s_om)
        p.wait_now("vector", s_om)
        p.emit("vector", lambda e: e.tensor_copy(brep_sb[:], ps[:, 5, 0:128]), inc=s_oe)
        for lt in range(32):
            for hl in range(4):
                p.wait_now("gpsimd", s_om)
                p.dma("gpsimd", och_sb[:].bitcast(BF16)[:, 0:128] if False else vc_sb[:],
                      oS[hl, lt * 128:(lt + 1) * 128, :], s_ol)
                p.dma("gpsimd", gg_sb[:], ggS[lt], s_ol)
                p.wait("scalar", s_ol, p.cnt[s_ol])
                p.wait("vector", s_ol, p.cnt[s_ol])
                p.emit("scalar", lambda e: e.activation(t2_sb[:], vc_sb[:], AF.Square,
                                                        accum_out=cols_sb[:, 0:1]), inc=s_cmS)
                p.wait_now("vector", s_cmS)
                p.emit("vector", lambda e: e.tensor_scalar_mul(cols_sb[:, 1:2], cols_sb[:, 0:1],
                                                               float(1.0 / 128.0)), inc=s_cmV)
                p.wait_now("scalar", s_cmV)
                p.emit("scalar", lambda e: e.activation(cols_sb[:, 2:3], cols_sb[:, 1:2], AF.Ln,
                                                        bias=cons_sb[:, 3:4]))
                p.emit("scalar", lambda e: e.activation(cols_sb[:, 2:3], cols_sb[:, 2:3], AF.Exp,
                                                        scale=cons_sb[:, 2:3]), inc=s_cmS)
                p.wait_now("vector", s_cmS)
                p.emit("vector", lambda e: e.tensor_scalar(
                    och_sb[:], vc_sb[:], cols_sb[:, 2:3], None, ALU.mult, ALU.bypass))
                p.emit("vector", lambda e: e.tensor_mul(och_sb[:], och_sb[:], brep_sb[:]),
                       inc=s_cmV)
                p.wait_now("scalar", s_cmV)
                p.emit("scalar", lambda e, hl=hl: e.activation(
                    t2_sb[:], gg_sb[:, hl * 128:(hl + 1) * 128], AF.Silu), inc=s_cmS)
                p.wait_now("vector", s_cmS)
                p.emit("vector", lambda e: e.tensor_mul(och_sb[:], och_sb[:], t2_sb[:]),
                       inc=s_cmV)
                p.wait_now("tensor", s_cmV)
                p.wait_now("tensor", s_oe)
                p.emit("tensor", lambda e: e.transpose(ps[:, 4, 0:128], och_sb[:], idf_sb[:]),
                       inc=s_om)
                p.wait("vector", s_om, p.cnt[s_om])
                p.emit("vector", lambda e, hl=hl: e.tensor_copy(ot_sb[:, hl, :], ps[:, 4, 0:128]),
                       inc=s_oe)
            for dg in range(16):
                bank = dg % 4
                p.wait_now("tensor", s_oe)
                for hch in range(4):
                    p.emit("tensor", lambda e, dg=dg, hch=hch, bank=bank: e.matmul(
                        ps[:, bank, 0:128], wo_sb[:, dg // 4, hch, (dg % 4) * 128:(dg % 4 + 1) * 128],
                        ot_sb[:, hch, :], start=(hch == 0), stop=(hch == 3)),
                        inc=(s_om if hch == 3 else None))
                p.wait("vector", s_om, p.cnt[s_om])
                p.emit("vector", lambda e, dg=dg, bank=bank: e.tensor_copy(
                    po_sb[:, dg, :], ps[:, bank, 0:128]), inc=s_oe)
            p.wait_now("sync", s_oe)
            if lt >= 1:
                p.wait("sync", s_fst, 16 * lt)
            p.dma("sync", pl3[:, :, lt * 128:(lt + 1) * 128], po_sb[:], s_fst)
            p.wait_now("vector", s_fst)

        for e in ENGINES:
            p.wait_now(e, s_fst)
        with nc.Block() as blk:
            p.build(blk)
    return nc


_BACKEND = "dev"


def _run(nc, in_maps, out_names):
    if _BACKEND == "dev":
        res = run_bass_kernel_spmd(nc, in_maps, core_ids=list(range(len(in_maps))))
        return [{n: np.asarray(res.results[c][n]) for n in out_names}
                for c in range(len(in_maps))]
    from concourse.bass_interp import CoreSim
    outs = []
    for im in in_maps:
        sim = CoreSim(nc)
        for k, v in im.items():
            sim.tensor(k)[:] = v
        sim.simulate(check_with_hw=False)
        outs.append({n: np.array(sim.tensor(n)) for n in out_names})
    return outs


def run_l3(s2, Wq, Wk, Wv, Wb, Wa, Wg, Wo, A_log, dt_bias, conv_q, conv_k,
           conv_v, o_norm_w):
    nc = _get("l3", build_l3)

    def colslice(W, h):
        return np.ascontiguousarray(W[h * 128:(h + 1) * 128].T)

    def pos(i):
        return 2 * (i % 64) + (i // 64)
    P = np.array([[pos(jj) < pos(ii) for jj in range(128)] for ii in range(128)])
    PI = np.array([[pos(jj) <= pos(ii) for jj in range(128)] for ii in range(128)])
    consts = {
        "mskSN": (-1.0 * P).astype(np.float32),
        "mskSTN": (-1.0 * P.T).astype(np.float32),
        "mskIT": PI.T.astype(np.float32),
        "idf": np.eye(128, dtype=np.float32),
        "idb": np.eye(128, dtype=np.float32),
        "onw": o_norm_w.reshape(1, 128).astype(np.float32),
    }
    in_maps = []
    for c in range(N_CORES):
        b, hh = c // 2, c % 2
        heads = list(range(4 * hh, 4 * hh + 4))
        wqT = np.concatenate([colslice(Wq, g) for g in heads], 1)
        wkT = np.concatenate([colslice(Wk, j * 8 + g) for j in range(2) for g in heads], 1)
        wvT = np.concatenate([colslice(Wv, j * 8 + g) for j in range(2) for g in heads], 1)
        wgT = np.concatenate([colslice(Wg, g) for g in heads], 1)
        wbT = np.ascontiguousarray(Wb[[j * 8 + g for j in range(2) for g in heads]].T)
        waT = np.ascontiguousarray(Wa[heads].T)
        woT = np.ascontiguousarray(Wo.T[hh * 512:(hh + 1) * 512])
        im = dict(consts)
        im.update({
            "s2f": pack_fm(np.ascontiguousarray(s2[:, b * L:(b + 1) * L])),
            "wqp": pack_lhsT(wqT, 512), "wkp": pack_lhsT(wkT, 512),
            "wvp": pack_lhsT(wvT, 512), "wgp": pack_lhsT(wgT, 512),
            "wbp": np.ascontiguousarray(wbT.reshape(16, 128, 8).transpose(1, 0, 2)),
            "wap": np.ascontiguousarray(waT.reshape(16, 128, 4).transpose(1, 0, 2)),
            "wop": pack_lhsT(woT, 512),
            "cvq": np.ascontiguousarray(np.concatenate(
                [conv_q[g * 128:(g + 1) * 128] for g in heads]).reshape(4, 128, 4).transpose(1, 0, 2)),
            "cvk": np.ascontiguousarray(np.concatenate(
                [conv_k[(j * 8 + g) * 128:(j * 8 + g + 1) * 128] for j in range(2) for g in heads]
            ).reshape(8, 128, 4).transpose(1, 0, 2)),
            "cvv": np.ascontiguousarray(np.concatenate(
                [conv_v[(j * 8 + g) * 128:(j * 8 + g + 1) * 128] for j in range(2) for g in heads]
            ).reshape(8, 128, 4).transpose(1, 0, 2)),
            "alg": A_log[heads].reshape(4, 1).astype(np.float32),
            "dtb": dt_bias[heads].reshape(4, 1).astype(np.float32),
        })
        in_maps.append(im)
    res = _run(nc, in_maps, ["pl3"])
    return [res[c]["pl3"] for c in range(N_CORES)]


def kernel(**inputs):
    import ml_dtypes as mld
    x = np.asarray(inputs["L0_attn_x"])
    y0 = run_mamba(x, np.asarray(inputs["L0_attn_Wx"]), np.asarray(inputs["L0_attn_Wdt"]),
                   np.asarray(inputs["L0_attn_bdt"]), np.asarray(inputs["L0_attn_A_log"]),
                   np.asarray(inputs["L0_attn_D"]))
    xfm = np.ascontiguousarray(x.reshape(TOK, D).T)
    s2 = run_l0ffn(xfm, y0, np.asarray(inputs["L0_ffn_W1"]), np.asarray(inputs["L0_ffn_W2"]),
                   np.asarray(inputs["L0_ffn_W3"]))
    pl = run_l3(s2, np.asarray(inputs["L1_attn_Wq"]), np.asarray(inputs["L1_attn_Wk"]),
                np.asarray(inputs["L1_attn_Wv"]), np.asarray(inputs["L1_attn_Wb"]),
                np.asarray(inputs["L1_attn_Wa"]), np.asarray(inputs["L1_attn_Wg"]),
                np.asarray(inputs["L1_attn_Wo"]), np.asarray(inputs["L1_attn_A_log"]),
                np.asarray(inputs["L1_attn_dt_bias"]), np.asarray(inputs["L1_attn_conv_q"]),
                np.asarray(inputs["L1_attn_conv_k"]), np.asarray(inputs["L1_attn_conv_v"]),
                np.asarray(inputs["L1_attn_o_norm_w"]))
    paps = [pl[2 * b] for b in range(N_CORES // 2)]
    pbps = [pl[2 * b + 1] for b in range(N_CORES // 2)]
    nc4a = _get("l4a", build_l4a)
    g1 = np.asarray(inputs["L1_ffn_ln1_scale"]).reshape(16, 128).T.astype(np.float32)
    b1 = np.asarray(inputs["L1_ffn_ln1_bias"]).reshape(16, 128).T.astype(np.float32)
    in_maps = []
    for c in range(N_CORES):
        b, half = c // 2, c % 2
        sl = np.s_[:, :, half * TPC:(half + 1) * TPC]
        in_maps.append({
            "s2i": pack_fm(np.ascontiguousarray(s2[:, c * TPC:(c + 1) * TPC])),
            "pai": np.ascontiguousarray(paps[b][sl]),
            "pbi": np.ascontiguousarray(pbps[b][sl]),
            "g1i": np.ascontiguousarray(g1), "b1i": np.ascontiguousarray(b1),
        })
    r4a = _run(nc4a, in_maps, ["s3o", "xno"])
    s3 = np.empty((D, TOK), np.float32)
    xn = np.empty((D, TOK), mld.bfloat16)
    for c in range(N_CORES):
        s3[:, c * TPC:(c + 1) * TPC] = unpack_fm(np.asarray(r4a[c]["s3o"]))
        xn[:, c * TPC:(c + 1) * TPC] = unpack_fm(np.asarray(r4a[c]["xno"]))
    nc4b = _get("l4b", build_l4b)
    tri = np.triu(np.full((128, 128), -1e9, np.float32), 1)
    wq = np.asarray(inputs["L1_ffn_wq"]); wk = np.asarray(inputs["L1_ffn_wk"])
    wv = np.asarray(inputs["L1_ffn_wv"]); wo = np.asarray(inputs["L1_ffn_wo"])
    in_maps = []
    for c in range(N_CORES):
        b, hh = c // 2, c % 2
        rsl = slice(hh * 1024, (hh + 1) * 1024)
        in_maps.append({
            "xni": pack_fm(np.ascontiguousarray(xn[:, b * L:(b + 1) * L])),
            "wqp": pack_lhsT(np.ascontiguousarray(wq[rsl].T), 512),
            "wkp": pack_lhsT(np.ascontiguousarray(wk[rsl].T), 512),
            "wvp": pack_lhsT(np.ascontiguousarray(wv[rsl].T), 512),
            "wop": pack_lhsT(np.ascontiguousarray(wo.T[rsl]), 512),
            "mko": tri, "mkt": np.ascontiguousarray(tri.T),
            "idf": np.eye(128, dtype=np.float32),
        })
    r4b = _run(nc4b, in_maps, ["po"])
    pa2 = [np.asarray(r4b[2 * b]["po"]) for b in range(N_CORES // 2)]
    pb2 = [np.asarray(r4b[2 * b + 1]["po"]) for b in range(N_CORES // 2)]
    nc4c = _get("l4c", build_l4c)
    g2 = np.asarray(inputs["L1_ffn_ln2_scale"]).reshape(16, 128).T.astype(np.float32)
    b2 = np.asarray(inputs["L1_ffn_ln2_bias"]).reshape(16, 128).T.astype(np.float32)
    sbv = np.asarray(inputs["L1_ffn_simple_bias"]).reshape(16, 128).T.astype(np.float32)
    f1p = pack_lhsT(np.ascontiguousarray(np.asarray(inputs["L1_ffn_w_ff1"]).T), 512)
    f2p = pack_lhsT(np.ascontiguousarray(np.asarray(inputs["L1_ffn_w_ff2"]).T), 256)
    wdT = np.ascontiguousarray(np.asarray(inputs["L1_ffn_w_down"]).T)
    wuT = np.ascontiguousarray(np.asarray(inputs["L1_ffn_w_up"]).T)
    wrT = np.ascontiguousarray(np.asarray(inputs["L1_ffn_w_router"]).T)
    in_maps = []
    for c in range(N_CORES):
        b, half = c // 2, c % 2
        sl = np.s_[:, :, half * TPC:(half + 1) * TPC]
        in_maps.append({
            "s3i": pack_fm(np.ascontiguousarray(s3[:, c * TPC:(c + 1) * TPC])),
            "pai": np.ascontiguousarray(pa2[b][sl]),
            "pbi": np.ascontiguousarray(pb2[b][sl]),
            "f1p": f1p, "f2p": f2p,
            "g2i": g2, "b2i": b2, "sbi": sbv,
            "wdp": np.ascontiguousarray(wdT.reshape(16, 128, 32).transpose(1, 0, 2)),
            "wup": np.ascontiguousarray(wuT.reshape(32, 16, 128)),
            "wrt": np.ascontiguousarray(wrT.reshape(16, 128, 3).transpose(1, 0, 2)),
        })
    r4c = _run(nc4c, in_maps, ["s4o"])
    s4 = np.empty((D, TOK), np.float32)
    for c in range(N_CORES):
        s4[:, c * TPC:(c + 1) * TPC] = unpack_fm(np.asarray(r4c[c]["s4o"]))
    return np.ascontiguousarray(s4.T.reshape(B, L, D)).astype(np.float32)
